# version 01
# baseline (speedup 1.0000x reference)
import contextlib
import numpy as np
import ml_dtypes
import concourse.bass as bass
import concourse.mybir as mybir
from concourse.bass_utils import run_bass_kernel_spmd

F32 = mybir.dt.float32
BF16 = mybir.dt.bfloat16
I32 = mybir.dt.int32
AF = mybir.ActivationFunctionType
ALU = mybir.AluOpType
BF = ml_dtypes.bfloat16

D = 2048
L = 8192
DIN = 2048
NST = 32
DTRK = 128
DFF = 5632
HFF = 8192
EPS = 1e-5

NT = 16
TC = 512
NCK = L // TC
TSC = 1024
NSC = L // TSC
SC = 512
NOC = 4096 // SC
NJT = 8
WA_OF = 1664
WA_OT = 13

DEBUG = ()
PHASES = ("fold", "p01", "p2", "p3a", "p3b", "p3c", "p3d", "p3e")
P2_LIM = None

ENGS = ("sync", "tensor", "vector", "scalar", "gpsimd")


class Rec:
    def __init__(self):
        self.items = {e: [] for e in ENGS}
        self.items["sync_pre"] = []
        self.items["gpsimd_pre"] = []

    def call(self, eng, fn, *a, inc=None, **kw):
        self.items[eng].append((fn, a, kw, inc))

    def raw(self, eng, f):
        self.items[eng].append((None, f, None, None))

    def wait(self, eng, sem, val):
        if val > 0:
            self.call(eng, "wait_ge", sem, val)

    def play(self, name, eng):
        for fn, a, kw, inc in self.items[name]:
            if fn is None:
                a(eng)
            else:
                r = getattr(eng, fn)(*a, **kw)
                if inc is not None:
                    r.then_inc(*inc)


def rl(dst_ap, src_ap, width):
    return dict(out=dst_ap.rearrange("p (i w) -> p i w", w=width),
                in_=src_ap.rearrange("(i p) w -> p i w", p=128))


def build_nc(phases=PHASES, debug=()):
    nc = bass.Bass("TRN2", target_bir_lowering=False, debug=False)

    def din(name, shape, dt=F32):
        return nc.dram_tensor(name, list(shape), dt, kind="ExternalInput").ap()

    x0b = din("x0b", (D, L))
    wa = din("wa", (D, WA_OF))
    wz = din("wz", (D, D))
    wdtp = din("wdtp", (DTRK, 1024))
    wop = din("wop", (DIN, D))
    wgate = din("wgate", (D, DFF))
    wupp = din("wupp", (D, DFF))
    wdown = din("wdown", (DFF, D))
    w1t = din("w1t", (D, HFF))
    w3t = din("w3t", (D, HFF))
    w2t = din("w2t", (HFF, D))
    h128 = din("h128", (128, 128))
    h64 = din("h64", (64, 64))
    alog = din("alog", (1024, NST))
    dtbT = din("dtbT", (128, 8))
    dparT = din("dparT", (128, 8))
    n1wT = din("n1wT", (128, 16))
    n2wT = din("n2wT", (128, 16))
    onesc = din("onesc", (128, 1))
    rndf = din("rndf", (1, 16), I32)

    x2o = nc.dram_tensor("x2o", [D, 4096], F32, kind="ExternalOutput").ap()

    def dt_(name, shape, dt=F32, shared=False):
        return nc.dram_tensor(name, list(shape), dt, kind="Internal",
                              addr_space="Shared" if shared else "Local").ap()

    x_dr = dt_("x_dr", (256, L))
    b_dr = dt_("b_dr", (256, L))
    c_dr = dt_("c_dr", (1024, L))
    dt_dr = dt_("dt_dr", (1024, L))
    sz_dr = dt_("sz_dr", (D, L), BF16)
    mr_dr = dt_("mr_dr", (2, L))
    row_dr = dt_("row_dr", (4, L))
    xn_dr = dt_("xn_dr", (D, 4096), BF16)
    hx_dr = dt_("hx_dr", (D, 4096))
    x1_dr = dt_("x1_dr", (D, 4096))
    ln2_dr = dt_("ln2_dr", (D, 4096), BF16)
    mh_dr = dt_("mh_dr", (DFF, 4096), BF16)
    fh_dr = dt_("fh_dr", (HFF, 4096), BF16)
    m1_dr = dt_("m1_dr", (HFF, D), BF16)
    w2p_dr = dt_("w2p_dr", (HFF, D), BF16)
    yd_sh = dt_("yd_sh", (2, 1024, L), shared=True)
    fl_sh = dt_("fl_sh", (2, 8), I32, shared=True)

    dbg = {}
    for name, src in (("dt", dt_dr), ("bc", b_dr), ("cc", c_dr), ("xs", x_dr),
                      ("sz", sz_dr), ("hx", hx_dr), ("x1", x1_dr),
                      ("ln2", ln2_dr), ("mh", mh_dr), ("fh", fh_dr),
                      ("w2p", w2p_dr), ("xn", xn_dr), ("mr", mr_dr),
                      ("yd", yd_sh)):
        if name in debug:
            dbg[name] = (nc.dram_tensor("dbg_" + name, list(src.shape),
                                        src.dtype, kind="ExternalOutput").ap(),
                         src)

    es = contextlib.ExitStack()
    es.__enter__()

    def sbt(name, p, w, dt=F32):
        return es.enter_context(nc.sbuf_tensor(name, [p, w], dt))

    a_sb = sbt("a_sb", 128, NJT * NST)
    dtb_sb = sbt("dtb_sb", 128, 8)
    dpar_sb = sbt("dpar_sb", 128, 8)
    n1w_sb = sbt("n1w_sb", 128, 16)
    n2w_sb = sbt("n2w_sb", 128, 16)
    ones_sb = sbt("ones_sb", 128, 1, BF16)
    h128_sb = sbt("h128_sb", 128, 128, BF16)
    h64_sb = sbt("h64_sb", 64, 64, BF16)
    carry = sbt("carry", 128, 256)
    rnd_sb = sbt("rnd_sb", 1, 16, I32)
    cb_sb = sbt("cb_sb", 128, 2)
    pf_sb = sbt("pf_sb", 1, 1, I32)
    mrow = sbt("mrow", 1, TC)
    msrow = sbt("msrow", 1, TC)
    vrow = sbt("vrow", 1, TC)
    rrow = sbt("rrow", 1, TC)

    POOLW = 48500
    pool = sbt("pool", 128, POOLW)

    class Carver:
        def __init__(self):
            self.off = 0

        def f32(self, w):
            assert self.off + w <= POOLW, f"pool overflow {self.off}+{w}"
            ap = pool[:, self.off:self.off + w]
            self.off += w
            return ap

        def bf16(self, w):
            assert w % 2 == 0
            return self.f32(w // 2).bitcast(BF16)

    pb = [es.enter_context(nc.psum_tensor(f"pb{i}", [128, 512], F32))
          for i in range(8)]

    sems = {s: es.enter_context(nc.semaphore(s))
            for s in ("qs", "qg", "pe", "ve", "ae", "po", "bar")}
    QS, QG, PE, VE, AE, PO, BAR = (sems[s] for s in
                                   ("qs", "qg", "pe", "ve", "ae", "po", "bar"))

    rec = Rec()
    C = {"qs": 0, "qg": 0, "pe": 0, "ve": 0, "ae": 0, "bar": 0}
    RT = {}

    _worig = rec.wait

    def _wait_fix(eng, sem, val):
        if sem is QS:
            val = C["qs"]
        elif sem is QG:
            val = C["qg"]
        _worig(eng, sem, val)
    rec.wait = _wait_fix

    def dmas(out, in_, **kw):
        _worig("sync", QS, C["qs"])
        C["qs"] += 16
        rec.call("sync", "dma_start", out=out, in_=in_, inc=(QS, 16), **kw)
        return C["qs"]

    def dmag(out, in_, **kw):
        _worig("gpsimd", QG, C["qg"])
        C["qg"] += 16
        rec.call("gpsimd", "dma_start", out=out, in_=in_, inc=(QG, 16), **kw)
        return C["qg"]

    def vop(fn, *a, **kw):
        _worig("vector", VE, C["ve"])
        C["ve"] += 1
        rec.call("vector", fn, *a, inc=(VE, 1), **kw)
        return C["ve"]

    def aop(fn, *a, **kw):
        _worig("scalar", AE, C["ae"])
        C["ae"] += 1
        rec.call("scalar", fn, *a, inc=(AE, 1), **kw)
        return C["ae"]

    def mmi(out, lhsT, rhs, start, stop, inc=False):
        if inc:
            C["pe"] += 1
            rec.call("tensor", "matmul", out, lhsT, rhs, start=start,
                     stop=stop, inc=(PE, 1))
        else:
            rec.call("tensor", "matmul", out, lhsT, rhs, start=start,
                     stop=stop)
        return C["pe"]

    def barrier():
        rec.wait("sync", QS, C["qs"])
        rec.wait("gpsimd", QG, C["qg"])
        C["bar"] += len(ENGS)
        tgt = C["bar"]
        for e in ENGS:
            rec.call(e, "sem_inc", BAR, 1)
            rec.wait(e, BAR, tgt)

    def pre_sync(e):
        pid = e.partition_id()
        hreg = e.alloc_register("h_sp")
        e.reg_alu(hreg, pid, 2, ALU.mod)
        RT["s_hv"] = bass.RuntimeValue(hreg, min_val=0, max_val=1)

    def pre_gp(e):
        pid = e.partition_id()
        hreg = e.alloc_register("h_gp")
        e.reg_alu(hreg, pid, 2, ALU.mod)
        RT["g_hv"] = bass.RuntimeValue(hreg, min_val=0, max_val=1)
        RT["g_cnt"] = e.alloc_register("pollcnt")
        RT["g_fv"] = e.alloc_register("pollfv")
        RT["g_want"] = e.alloc_register("pollwant")
        e.reg_mov(RT["g_cnt"], 0)

    rec.raw("sync_pre", pre_sync)
    rec.raw("gpsimd_pre", pre_gp)

    def dyn_dma_s(tensor_ap, dims, offs_fn, other, is_store):
        C["qs"] += 16

        wait_prev = C["qs"] - 16

        def run(e, tensor_ap=tensor_ap, dims=dims, offs_fn=offs_fn,
                other=other, is_store=is_store, wait_prev=wait_prev):
            if wait_prev > 0:
                e.wait_ge(QS, wait_prev)
            ap = bass.AP(tensor_ap.tensor, offs_fn(RT["h_arm"]), dims)
            if is_store:
                e.dma_start(out=ap, in_=other).then_inc(QS, 16)
            else:
                e.dma_start(out=other, in_=ap).then_inc(QS, 16)
        rec.raw("sync", run)
        return C["qs"]

    dmag(ones_sb[:], onesc)
    dmag(h128_sb[:], h128)
    dmag(h64_sb[:], h64)
    dmas(dtb_sb[:], dtbT)
    dmas(dpar_sb[:], dparT)
    dmas(n1w_sb[:], n1wT)
    dmas(n2w_sb[:], n2wT)
    dmas(rnd_sb[:], rndf)
    cvc = Carver()
    atmp = cvc.f32(NJT * NST)
    ld = dmas(**rl(atmp, alog, NST))
    rec.wait("scalar", QS, ld)
    aop("activation", a_sb[:], atmp, AF.Exp)
    rec.wait("vector", AE, C["ae"])
    vop("tensor_scalar_mul", a_sb[:], a_sb[:], -1.0)
    vop("memset", carry[:], 0.0)
    vop("memset", cb_sb[:, 0:1], EPS)
    vop("memset", cb_sb[:, 1:2], 1.0)
    nc.const_aps.register(F32, EPS, cb_sb[:, 0:1])
    nc.const_aps.register(F32, 1.0, cb_sb[:, 1:2])
    barrier()

    if "fold" in phases:
        cv = Carver()
        t1b = cv.bf16(2048)
        stg = [cv.bf16(512), cv.bf16(512)]
        t2b = cv.bf16(2048)
        stg2 = [cv.bf16(512), cv.bf16(512)]
        fold_scale = float(1.0 / np.sqrt(HFF))

        slot_dma = [0, 0]
        slot_ae = [0, 0]
        ns = 0
        for hl in range(64):
            w2src = bass.AP(w2t.tensor, hl * 2048, [[64 * 2048, 128], [1, 2048]])
            if hl >= 1:
                rec.wait("gpsimd", PE, C["pe"])
            lda = dmag(t1b, w2src)
            rec.wait("tensor", QG, lda)
            for dc in range(4):
                p = ns % 2
                bank = pb[p]
                rec.wait("tensor", AE, slot_ae[p])
                mmi(bank[:], h128_sb[:], t1b[:, dc * 512:(dc + 1) * 512],
                    True, True, inc=True)
                rec.wait("scalar", PE, C["pe"])
                rec.wait("scalar", QS, slot_dma[p])
                slot_ae[p] = aop("copy", stg[p], bank[:])
                m1dst = bass.AP(m1_dr.tensor, hl * 2048 + dc * 512,
                                [[64 * 2048, 128], [1, 512]])
                rec.wait("sync", AE, slot_ae[p])
                slot_dma[p] = dmas(m1dst, stg[p])
                ns += 1
        rec.wait("gpsimd", QS, C["qs"])
        slot_dma = [0, 0]
        slot_ae = [0, 0]
        ns = 0
        for hh in range(128):
            m1src = bass.AP(m1_dr.tensor, hh * 64 * 2048, [[2048, 64], [1, 2048]])
            if hh >= 1:
                rec.wait("gpsimd", PE, C["pe"])
            lda = dmag(t2b[0:64, :], m1src)
            rec.wait("tensor", QG, lda)
            for dc in range(4):
                p = ns % 2
                bank = pb[2 + p]
                rec.wait("tensor", AE, slot_ae[p])
                mmi(bank[0:64, :], h64_sb[:], t2b[0:64, dc * 512:(dc + 1) * 512],
                    True, True, inc=True)
                rec.wait("scalar", PE, C["pe"])
                rec.wait("scalar", QS, slot_dma[p])
                slot_ae[p] = aop("mul", stg2[p][0:64, :], bank[0:64, :],
                                 fold_scale)
                w2pdst = bass.AP(w2p_dr.tensor, hh * 64 * 2048 + dc * 512,
                                 [[2048, 64], [1, 512]])
                rec.wait("sync", AE, slot_ae[p])
                slot_dma[p] = dmas(w2pdst, stg2[p][0:64, :])
                ns += 1
        barrier()

    if "p01" in phases:
        cv = Carver()
        wa_sb = cv.bf16(NT * WA_OF)
        wz_sb = cv.bf16(NT * 2048)
        wdt_sb = cv.bf16(1024)
        x0bf = cv.f32(NT * TC)
        ubuf = cv.bf16(NT * TC)
        xb2 = [cv.bf16(TC), cv.bf16(TC)]
        dtrb = cv.bf16(TC)
        stg = [cv.f32(TC), cv.f32(TC), cv.f32(TC)]
        m_b = cv.f32(TC)
        r_b = cv.f32(TC)
        rr_b = cv.f32(TC)
        e1 = cv.f32(TC)
        e2 = cv.f32(TC)

        dmag(**rl(wa_sb, wa, WA_OF))
        dmag(**rl(wz_sb, wz, 2048))
        ldw_done = dmag(wdt_sb, wdtp)

        def wa_st(ki, ot):
            return wa_sb[:, ki * WA_OF + ot * 128:ki * WA_OF + (ot + 1) * 128]

        def wz_st(ki, ot):
            return wz_sb[:, ki * 2048 + ot * 128:ki * 2048 + (ot + 1) * 128]

        slot_dma = [0, 0, 0]
        bank_rel = [0, 0, 0, 0]
        bank_rel_ve = [0, 0, 0, 0]
        for k in range(NCK):
            t0 = k * TC
            rec.wait("sync", VE, C["ve"])
            rec.wait("sync", PE, C["pe"])
            xld = dmas(**rl(x0bf, x0b[:, t0:t0 + TC], TC))
            for ti in range(NT):
                rec.wait("scalar", QS, xld)
                rec.wait("scalar", PE, C["pe"])
                aop("copy", xb2[0], x0bf[:, ti * TC:(ti + 1) * TC])
                a2 = aop("activation", xb2[1], x0bf[:, ti * TC:(ti + 1) * TC],
                         AF.Square)
                rec.wait("tensor", AE, a2)
                if ti == 0:
                    rec.wait("tensor", VE, C["ve"])
                mmi(pb[6][0:1, :], ones_sb[:], xb2[0], ti == 0, ti == NT - 1)
                mmi(pb[7][0:1, :], ones_sb[:], xb2[1], ti == 0, ti == NT - 1,
                    inc=True)
            rec.wait("vector", PE, C["pe"])
            vop("tensor_scalar_mul", mrow[:], pb[6][0:1, :], 1.0 / D)
            vop("tensor_scalar_mul", msrow[:], pb[7][0:1, :], 1.0 / D)
            vop("tensor_tensor", vrow[:], mrow[:], mrow[:], ALU.mult)
            v_var = vop("tensor_tensor", vrow[:], msrow[:], vrow[:],
                        ALU.subtract)
            rec.wait("scalar", VE, v_var)
            aop("activation", rrow[:], vrow[:], AF.Ln, bias=EPS)
            a_r = aop("activation", rrow[:], rrow[:], AF.Exp, scale=-0.5)
            rec.wait("sync", VE, v_var)
            rec.wait("sync", AE, a_r)
            dmas(mr_dr[0:1, t0:t0 + TC], mrow[:])
            dmas(mr_dr[1:2, t0:t0 + TC], rrow[:])
            rec.wait("sync", QS, C["qs"])
            dmas(m_b, mr_dr[0:1, t0:t0 + TC].partition_broadcast(128))
            bc1 = dmas(r_b, mr_dr[1:2, t0:t0 + TC].partition_broadcast(128))
            for ti in range(NT):
                rec.wait("vector", QS, bc1)
                sl = x0bf[:, ti * TC:(ti + 1) * TC]
                vop("tensor_tensor", sl, sl, m_b, ALU.subtract)
                vln = vop("tensor_tensor", sl, sl, r_b, ALU.mult)
                rec.wait("scalar", VE, vln)
                rec.wait("scalar", PE, C["pe"])
                a3 = aop("activation", xb2[ti % 2], sl, AF.Square)
                rec.wait("tensor", AE, a3)
                if ti == 0:
                    rec.wait("tensor", VE, C["ve"])
                mmi(pb[5][0:1, :], ones_sb[:], xb2[ti % 2], ti == 0,
                    ti == NT - 1, inc=True)
            rec.wait("vector", PE, C["pe"])
            v_s2 = vop("tensor_scalar_mul", vrow[:], pb[5][0:1, :], 1.0 / D)
            rec.wait("scalar", VE, v_s2)
            aop("activation", rrow[:], vrow[:], AF.Ln, bias=EPS)
            a_rr = aop("activation", rrow[:], rrow[:], AF.Exp, scale=-0.5)
            rec.wait("sync", AE, a_rr)
            dmas(row_dr[0:1, t0:t0 + TC], rrow[:])
            rec.wait("sync", QS, C["qs"])
            bc2 = dmas(rr_b, row_dr[0:1, t0:t0 + TC].partition_broadcast(128))
            rec.wait("vector", PE, C["pe"])
            for ti in range(NT):
                rec.wait("vector", QS, bc2)
                u_done = vop("scalar_tensor_tensor",
                             ubuf[:, ti * TC:(ti + 1) * TC],
                             x0bf[:, ti * TC:(ti + 1) * TC],
                             n1w_sb[:, ti:ti + 1], rr_b, ALU.mult, ALU.mult)
            rec.wait("tensor", VE, u_done)
            if k == 0:
                rec.wait("tensor", QG, ldw_done)

            def u_mv(ki):
                return ubuf[:, ki * TC:(ki + 1) * TC]

            state = {"g": 0}

            def ogroup(stf, nof, evac):
                for ot in range(nof):
                    bi = state["g"] % 4
                    rec.wait("tensor", AE, bank_rel[bi])
                    rec.wait("tensor", VE, bank_rel_ve[bi])
                    for ki in range(NT):
                        mmi(pb[bi][:], stf(ki, ot), u_mv(ki), ki == 0,
                            ki == NT - 1, inc=(ki == NT - 1))
                    evac(pb[bi], ot, bi)
                    state["g"] += 1

            def evac_wa(bank, ot, bi):
                rec.wait("scalar", PE, C["pe"])
                if ot == 12:
                    bank_rel[bi] = aop("copy", dtrb, bank[:])
                else:
                    si = ot % 3
                    rec.wait("scalar", QS, slot_dma[si])
                    a = aop("copy", stg[si], bank[:])
                    if ot < 2:
                        dst = x_dr[ot * 128:(ot + 1) * 128, t0:t0 + TC]
                    elif ot < 4:
                        dst = b_dr[(ot - 2) * 128:(ot - 1) * 128, t0:t0 + TC]
                    else:
                        dst = c_dr[(ot - 4) * 128:(ot - 3) * 128, t0:t0 + TC]
                    rec.wait("sync", AE, a)
                    slot_dma[si] = dmas(dst, stg[si])
                    bank_rel[bi] = a

            def evac_wz(bank, ot, bi):
                rec.wait("scalar", PE, C["pe"])
                si = ot % 3
                rec.wait("scalar", QS, slot_dma[si])
                s = stg[si].bitcast(BF16)[:, 0:TC]
                a = aop("activation", s, bank[:], AF.Silu)
                rec.wait("sync", AE, a)
                slot_dma[si] = dmas(sz_dr[ot * 128:(ot + 1) * 128, t0:t0 + TC],
                                    s)
                bank_rel[bi] = a

            ogroup(wa_st, WA_OT, evac_wa)
            ogroup(wz_st, NT, evac_wz)
            rec.wait("tensor", AE, C["ae"])
            for ot in range(8):
                bi = state["g"] % 4
                rec.wait("tensor", AE, bank_rel[bi])
                rec.wait("tensor", VE, bank_rel_ve[bi])
                mmi(pb[bi][:], wdt_sb[:, ot * 128:(ot + 1) * 128], dtrb,
                    True, True, inc=True)
                state["g"] += 1
                rec.wait("vector", PE, C["pe"])
                rec.wait("vector", AE, C["ae"])
                v1 = vop("tensor_scalar_add", e1, pb[bi][:],
                         dtb_sb[:, ot:ot + 1])
                bank_rel_ve[bi] = v1
                rec.wait("scalar", VE, v1)
                aop("activation", e2, e1, AF.Abs)
                aop("activation", e2, e2, AF.Exp, scale=-1.0)
                a4 = aop("activation", e2, e2, AF.Ln, bias=1.0)
                rec.wait("vector", AE, a4)
                si = ot % 3
                rec.wait("vector", QS, slot_dma[si])
                v2 = vop("scalar_tensor_tensor", stg[si], e1, 0.0, e2,
                         ALU.max, ALU.add)
                rec.wait("sync", VE, v2)
                slot_dma[si] = dmas(
                    dt_dr[ot * 128:(ot + 1) * 128, t0:t0 + TC], stg[si])
        barrier()

    if "p2" in phases:
        cv = Carver()
        dtb2 = cv.f32(NJT * TSC)
        vbuf = cv.f32(NJT * TSC)
        xrep = cv.f32(TSC)
        dab = [cv.f32(TSC), cv.f32(TSC)]
        ub = cv.f32(TSC)
        hb = cv.f32(TSC)
        wb_ = cv.f32(TSC)
        yb = cv.f32(TSC)
        Bb = [cv.f32(TSC), cv.f32(TSC)]
        Cb = [cv.f32(TSC), cv.f32(TSC)]
        yst = [cv.f32(TSC), cv.f32(TSC)]

        bb_free = [0, 0]
        cb_free = [0, 0]
        da_free = [0, 0]
        yst_free = [0, 0]
        it = 0
        lim_s, lim_j, lim_n = P2_LIM if P2_LIM else (NSC, NJT, NST)
        for S in range(lim_s):
            ts = S * TSC
            rec.wait("sync", VE, C["ve"])
            dld = dmas(**rl(dtb2, dt_dr[:, ts:ts + TSC], TSC))
            for j in range(lim_j):
                dsl = dtb2[:, j * TSC:(j + 1) * TSC]
                vsl = vbuf[:, j * TSC:(j + 1) * TSC]
                rec.wait("gpsimd", VE, C["ve"])
                xsrc = bass.AP(x_dr.tensor, (j * 32) * L + ts,
                               [[0, 4], [L, 32], [1, TSC]])
                xld = dmag(xrep, xsrc)
                rec.wait("vector", QS, dld)
                rec.wait("vector", QG, xld)
                vop("tensor_tensor", vsl, dsl, xrep, ALU.mult)
                for n in range(lim_n):
                    p = it % 2
                    idx = j * NST + n
                    rec.wait("gpsimd", VE, bb_free[p])
                    bsrc = bass.AP(b_dr.tensor, (j * 32 + n) * L + ts,
                                   [[0, 128], [1, TSC]])
                    bld = dmag(Bb[p], bsrc)
                    rec.wait("gpsimd", VE, cb_free[p])
                    csrc = bass.AP(c_dr.tensor, (j * 128 + n) * L + ts,
                                   [[32 * L, 4], [0, 32], [1, TSC]])
                    cld = dmag(Cb[p], csrc)
                    rec.wait("scalar", QS, dld)
                    rec.wait("scalar", VE, da_free[p])
                    a_da = aop("activation", dab[p], dsl, AF.Exp,
                               scale=a_sb[:, idx:idx + 1])
                    rec.wait("vector", QG, bld)
                    bb_free[p] = vop("tensor_tensor", ub, vsl, Bb[p], ALU.mult)
                    rec.wait("vector", AE, a_da)
                    da_free[p] = vop("tensor_tensor_scan", hb, dab[p], ub,
                                     carry[:, idx:idx + 1], ALU.mult, ALU.add)
                    vop("tensor_copy", carry[:, idx:idx + 1],
                        hb[:, TSC - 1:TSC])
                    rec.wait("vector", QG, cld)
                    if n == 0:
                        cb_free[p] = vop("tensor_tensor", yb, hb, Cb[p],
                                         ALU.mult)
                    else:
                        cb_free[p] = vop("tensor_tensor", wb_, hb, Cb[p],
                                         ALU.mult)
                        vop("tensor_tensor", yb, yb, wb_, ALU.add)
                    it += 1
                pj = j % 2
                rec.wait("vector", QS, yst_free[pj])
                v_yd = vop("scalar_tensor_tensor", yst[pj], xrep,
                           dpar_sb[:, j:j + 1], yb, ALU.mult, ALU.add)
                rec.wait("sync", VE, v_yd)
                yst_free[pj] = dyn_dma_s(
                    yd_sh, [[L, 128], [1, TSC]],
                    lambda h, j=j, ts=ts: h * 1024 * L + j * 128 * L + ts,
                    yst[pj], True)
            rec.wait("sync", QS, C["qs"])
            dyn_dma_s(fl_sh, [[1, 1], [1, 1]],
                      lambda h, S=S: h * 8 + S,
                      rnd_sb[0:1, S:S + 1], True)
        if "vb" in debug:
            vb_o = nc.dram_tensor("dbg_vb", [128, NJT * TSC], F32,
                                  kind="ExternalOutput").ap()
            rec.wait("sync", VE, C["ve"])
            dmas(vb_o, vbuf)
        if "cy" in debug:
            cy_o = nc.dram_tensor("dbg_cy", [128, 256], F32,
                                  kind="ExternalOutput").ap()
            rec.wait("sync", VE, C["ve"])
            dmas(cy_o, carry[:])
        barrier()

    if "p3a" in phases:
        cv = Carver()
        wop_sb = cv.bf16(NT * 2048)
        ydt = [cv.f32(SC), cv.f32(SC)]
        y2b = cv.bf16(NT * SC)
        x0c = cv.f32(NT * SC)
        hbuf = cv.f32(NT * SC)
        ztile = [cv.bf16(SC), cv.bf16(SC)]
        hsq = [cv.bf16(SC), cv.bf16(SC)]
        ttile = cv.f32(SC)
        hxs = [cv.f32(SC), cv.f32(SC)]
        xns = [cv.bf16(SC), cv.bf16(SC)]
        m_b = cv.f32(SC)
        r_b = cv.f32(SC)
        rr_b = cv.f32(SC)

        lwop = dmag(**rl(wop_sb, wop, 2048))
        bank_free = [0, 0, 0, 0]
        hx_dma = [0, 0]
        xn_dma = [0, 0]
        for i in range(NOC):
            def poll(e, i=i):
                cnt = RT["g_cnt"]
                fv = RT["g_fv"]
                want = RT["g_want"]
                oh = 1 - RT["h_arm_g"]
                e.load(want, rnd_sb[0:1, i:i + 1])
                e.reg_add(cnt, cnt, 16)
                e.dma_start(out=pf_sb[:], in_=fl_sh[oh:oh + 1, i:i + 1]
                            ).then_inc(PO, 16)
                e.wait_ge(PO, cnt)
                e.load(fv, pf_sb[0:1, 0:1])
                with e.While(bass.RuntimeValue(fv) != bass.RuntimeValue(want)):
                    e.reg_add(cnt, cnt, 16)
                    e.dma_start(out=pf_sb[:], in_=fl_sh[oh:oh + 1, i:i + 1]
                                ).then_inc(PO, 16)
                    e.wait_ge(PO, cnt)
                    e.load(fv, pf_sb[0:1, 0:1])
                e.sem_inc(QG, 16)
            C["qg"] += 16
            rec.raw("gpsimd", poll)
            pollc = C["qg"]

            rec.wait("sync", QG, pollc)
            rec.wait("sync", VE, C["ve"])
            rec.wait("sync", PE, C["pe"])

            dyn_dma_s(x0b, [[L, 128], [128 * L, 16], [1, SC]],
                      lambda h, i=i: h * 512 + 1024 * i,
                      x0c.rearrange("p (j t) -> p j t", t=SC), False)
            dyn_dma_s(mr_dr, [[0, 128], [1, SC]],
                      lambda h, i=i: h * 512 + 1024 * i, m_b, False)
            ld_main = dyn_dma_s(mr_dr, [[0, 128], [1, SC]],
                                lambda h, i=i: L + h * 512 + 1024 * i,
                                r_b, False)

            zcons = {}
            for ti in range(NT):
                if ti >= 2:
                    rec.wait("sync", VE, zcons[ti - 2])

                dyn_dma_s(sz_dr, [[L, 128], [1, SC]],
                          lambda h, i=i, ti=ti: ti * 128 * L + h * 512
                          + 1024 * i,
                          ztile[ti % 2], False)
                s_, jj = (0, ti) if ti < 8 else (1, ti - 8)
                dyn_dma_s(yd_sh, [[L, 128], [1, SC]],
                          lambda h, i=i, s_=s_, jj=jj: s_ * 1024 * L
                          + jj * 128 * L + h * 512 + 1024 * i,
                          ydt[ti % 2], False)
                rec.wait("vector", QS, C["qs"])
                if ti == 0:
                    rec.wait("vector", QS, ld_main)
                    rec.wait("vector", PE, C["pe"])
                zcons[ti] = vop("tensor_tensor",
                                y2b[:, ti * SC:(ti + 1) * SC],
                                ydt[ti % 2], ztile[ti % 2],
                                ALU.mult)
            y2done = zcons[NT - 1]

            rec.wait("tensor", VE, y2done)
            if i == 0:
                rec.wait("tensor", QG, lwop)
            for ot in range(NT):
                bi = ot % 4
                rec.wait("tensor", VE, bank_free[bi])
                for ki in range(NT):
                    mmi(pb[bi][:],
                        wop_sb[:, ki * 2048 + ot * 128:ki * 2048 + (ot + 1) * 128],
                        y2b[:, ki * SC:(ki + 1) * SC], ki == 0, ki == NT - 1,
                        inc=(ki == NT - 1))
                x0sl = x0c[:, ot * SC:(ot + 1) * SC]
                hsl = hbuf[:, ot * SC:(ot + 1) * SC]
                vop("tensor_tensor", ttile, x0sl, m_b, ALU.subtract)
                vop("tensor_tensor", ttile, ttile, r_b, ALU.mult)
                rec.wait("vector", PE, C["pe"])
                bank_free[bi] = vop("tensor_tensor", hsl, ttile, pb[bi][:],
                                    ALU.add)
                pj = ot % 2
                rec.wait("vector", QS, hx_dma[pj])
                v_hx = vop("tensor_tensor", hxs[pj], hsl, x0sl, ALU.add)
                rec.wait("sync", VE, v_hx)
                hx_dma[pj] = dmas(
                    hx_dr[ot * 128:(ot + 1) * 128, i * SC:(i + 1) * SC],
                    hxs[pj])
            for ti in range(NT):
                rec.wait("scalar", PE, C["pe"])
                a5 = aop("activation", hsq[ti % 2],
                         hbuf[:, ti * SC:(ti + 1) * SC], AF.Square)
                rec.wait("tensor", AE, a5)
                if ti == 0:
                    rec.wait("tensor", VE, C["ve"])
                mmi(pb[5][0:1, :], ones_sb[:], hsq[ti % 2], ti == 0,
                    ti == NT - 1, inc=True)
            rec.wait("vector", PE, C["pe"])
            v_s2 = vop("tensor_scalar_mul", vrow[:], pb[5][0:1, :], 1.0 / D)
            rec.wait("scalar", VE, v_s2)
            aop("activation", rrow[:], vrow[:], AF.Ln, bias=EPS)
            a_rr = aop("activation", rrow[:], rrow[:], AF.Exp, scale=-0.5)
            rec.wait("sync", AE, a_rr)
            dmas(row_dr[1:2, i * SC:(i + 1) * SC], rrow[:])
            rec.wait("sync", QS, C["qs"])
            bc3 = dmas(rr_b, row_dr[1:2, i * SC:(i + 1) * SC]
                       .partition_broadcast(128))
            for ti in range(NT):
                rec.wait("vector", QS, bc3)
                pj = ti % 2
                rec.wait("vector", QS, xn_dma[pj])
                v_xn = vop("scalar_tensor_tensor", xns[pj],
                           hbuf[:, ti * SC:(ti + 1) * SC],
                           n2w_sb[:, ti:ti + 1], rr_b, ALU.mult, ALU.mult)
                rec.wait("sync", VE, v_xn)
                xn_dma[pj] = dmas(
                    xn_dr[ti * 128:(ti + 1) * 128, i * SC:(i + 1) * SC],
                    xns[pj])
        barrier()

    if "p3b" in phases:
        cv = Carver()
        wg_sb = [cv.bf16(NT * 512), cv.bf16(NT * 512)]
        wu_sb = [cv.bf16(NT * 512), cv.bf16(NT * 512)]
        xnb = cv.bf16(NT * SC)
        sg = cv.f32(SC)
        mstg = [cv.bf16(SC), cv.bf16(SC)]
        NOB = DFF // 512
        wld = {}
        mst_free = [0, 0]
        for ob in range(NOB):
            p = ob % 2
            if ob >= 2:
                rec.wait("gpsimd", PE, wld[ob - 2])
            dmag(**rl(wg_sb[p], wgate[:, ob * 512:(ob + 1) * 512], 512))
            lw = dmag(**rl(wu_sb[p], wupp[:, ob * 512:(ob + 1) * 512], 512))
            for i in range(NOC):
                rec.wait("sync", PE, C["pe"])
                xld = dmas(**rl(xnb, xn_dr[:, i * SC:(i + 1) * SC], SC))
                rec.wait("tensor", QS, xld)
                rec.wait("tensor", QG, lw)
                for ot in range(4):
                    bg = pb[ot]
                    bu = pb[4 + ot]
                    rec.wait("tensor", AE, C["ae"])
                    rec.wait("tensor", VE, C["ve"])
                    for ki in range(NT):
                        mmi(bg[:],
                            wg_sb[p][:, ki * 512 + ot * 128:ki * 512 + (ot + 1) * 128],
                            xnb[:, ki * SC:(ki + 1) * SC], ki == 0,
                            ki == NT - 1)
                    for ki in range(NT):
                        mmi(bu[:],
                            wu_sb[p][:, ki * 512 + ot * 128:ki * 512 + (ot + 1) * 128],
                            xnb[:, ki * SC:(ki + 1) * SC], ki == 0,
                            ki == NT - 1, inc=(ki == NT - 1))
                    rec.wait("scalar", PE, C["pe"])
                    rec.wait("scalar", VE, C["ve"])
                    a6 = aop("activation", sg, bg[:], AF.Silu)
                    rec.wait("vector", AE, a6)
                    pj = (4 * i + ot) % 2
                    rec.wait("vector", QS, mst_free[pj])
                    v_h = vop("tensor_tensor", mstg[pj], sg, bu[:], ALU.mult)
                    rec.wait("sync", VE, v_h)
                    mst_free[pj] = dmas(
                        mh_dr[ob * 512 + ot * 128:ob * 512 + (ot + 1) * 128,
                              i * SC:(i + 1) * SC], mstg[pj])
                wld[ob] = C["pe"]
        barrier()

    if "p3c" in phases:
        cv = Carver()
        NHT = DFF // 128
        hidb = cv.bf16(NHT * SC)
        hxb = cv.f32(NT * SC)
        x1b = cv.f32(NT * SC)
        dw_sb = [cv.bf16(NHT * 128), cv.bf16(NHT * 128)]
        xb2c = [cv.bf16(SC), cv.bf16(SC)]
        l2s = [cv.bf16(SC), cv.bf16(SC)]
        x1s = [cv.f32(SC), cv.f32(SC)]
        m_b = cv.f32(SC)
        r_b = cv.f32(SC)
        wfree = [0, 0]
        l2_dma = [0, 0]
        x1_store = 0
        for i in range(NOC):
            rec.wait("sync", VE, C["ve"])
            rec.wait("sync", PE, C["pe"])
            hld = dmas(**rl(hidb, mh_dr[:, i * SC:(i + 1) * SC], SC))
            hxld = dmas(**rl(hxb, hx_dr[:, i * SC:(i + 1) * SC], SC))
            rec.wait("vector", QS, x1_store)
            for ot in range(NT):
                p = ot % 2
                rec.wait("gpsimd", PE, wfree[p])
                lw = dmag(**rl(dw_sb[p], wdown[:, ot * 128:(ot + 1) * 128],
                               128))
                bi = pb[ot % 4]
                rec.wait("tensor", QS, hld)
                rec.wait("tensor", QG, lw)
                rec.wait("tensor", VE, C["ve"])
                for ki in range(NHT):
                    mmi(bi[:], dw_sb[p][:, ki * 128:(ki + 1) * 128],
                        hidb[:, ki * SC:(ki + 1) * SC], ki == 0, ki == NHT - 1,
                        inc=(ki == NHT - 1))
                wfree[p] = C["pe"]
                rec.wait("vector", PE, C["pe"])
                rec.wait("vector", QS, hxld)
                vop("tensor_tensor", x1b[:, ot * SC:(ot + 1) * SC],
                    hxb[:, ot * SC:(ot + 1) * SC], bi[:], ALU.add)
            for ti in range(NT):
                rec.wait("scalar", PE, C["pe"])
                sl = x1b[:, ti * SC:(ti + 1) * SC]
                aop("copy", xb2c[0], sl)
                a2 = aop("activation", xb2c[1], sl, AF.Square)
                rec.wait("tensor", AE, a2)
                if ti == 0:
                    rec.wait("tensor", VE, C["ve"])
                mmi(pb[6][0:1, :], ones_sb[:], xb2c[0], ti == 0, ti == NT - 1)
                mmi(pb[7][0:1, :], ones_sb[:], xb2c[1], ti == 0, ti == NT - 1,
                    inc=True)
            rec.wait("vector", PE, C["pe"])
            vop("tensor_scalar_mul", mrow[:], pb[6][0:1, :], 1.0 / D)
            vop("tensor_scalar_mul", msrow[:], pb[7][0:1, :], 1.0 / D)
            vop("tensor_tensor", vrow[:], mrow[:], mrow[:], ALU.mult)
            v_var = vop("tensor_tensor", vrow[:], msrow[:], vrow[:],
                        ALU.subtract)
            rec.wait("scalar", VE, v_var)
            aop("activation", rrow[:], vrow[:], AF.Ln, bias=EPS)
            a_r = aop("activation", rrow[:], rrow[:], AF.Exp, scale=-0.5)
            rec.wait("sync", VE, v_var)
            rec.wait("sync", AE, a_r)
            dmas(row_dr[2:3, i * SC:(i + 1) * SC], mrow[:])
            dmas(row_dr[3:4, i * SC:(i + 1) * SC], rrow[:])
            rec.wait("sync", QS, C["qs"])
            dmas(m_b, row_dr[2:3, i * SC:(i + 1) * SC].partition_broadcast(128))
            bc4 = dmas(r_b, row_dr[3:4, i * SC:(i + 1) * SC]
                       .partition_broadcast(128))
            for ti in range(NT):
                pj = ti % 2
                sl = x1b[:, ti * SC:(ti + 1) * SC]
                rec.wait("vector", QS, bc4)
                rec.wait("vector", QS, l2_dma[pj])
                vop("tensor_tensor", x1s[pj], sl, m_b, ALU.subtract)
                v_l2 = vop("tensor_tensor", l2s[pj], x1s[pj], r_b, ALU.mult)
                rec.wait("sync", VE, v_l2)
                l2_dma[pj] = dmas(
                    ln2_dr[ti * 128:(ti + 1) * 128, i * SC:(i + 1) * SC],
                    l2s[pj])
            x1_store = dmas(x1_dr[:, i * SC:(i + 1) * SC]
                            .rearrange("(j p) t -> p j t", p=128),
                            x1b.rearrange("p (j t) -> p j t", t=SC))
        barrier()

    if "p3d" in phases:
        cv = Carver()
        wg_sb = [cv.bf16(NT * 512), cv.bf16(NT * 512)]
        wu_sb = [cv.bf16(NT * 512), cv.bf16(NT * 512)]
        lnb = cv.bf16(NT * SC)
        sg = cv.f32(SC)
        fstg = [cv.bf16(SC), cv.bf16(SC)]
        NOB = HFF // 512
        wld = {}
        fst_free = [0, 0]
        for ob in range(NOB):
            p = ob % 2
            if ob >= 2:
                rec.wait("gpsimd", PE, wld[ob - 2])
            dmag(**rl(wg_sb[p], w1t[:, ob * 512:(ob + 1) * 512], 512))
            lw = dmag(**rl(wu_sb[p], w3t[:, ob * 512:(ob + 1) * 512], 512))
            for i in range(NOC):
                rec.wait("sync", PE, C["pe"])
                xld = dmas(**rl(lnb, ln2_dr[:, i * SC:(i + 1) * SC], SC))
                rec.wait("tensor", QS, xld)
                rec.wait("tensor", QG, lw)
                for ot in range(4):
                    bg = pb[ot]
                    bu = pb[4 + ot]
                    rec.wait("tensor", AE, C["ae"])
                    rec.wait("tensor", VE, C["ve"])
                    for ki in range(NT):
                        mmi(bg[:],
                            wg_sb[p][:, ki * 512 + ot * 128:ki * 512 + (ot + 1) * 128],
                            lnb[:, ki * SC:(ki + 1) * SC], ki == 0,
                            ki == NT - 1)
                    for ki in range(NT):
                        mmi(bu[:],
                            wu_sb[p][:, ki * 512 + ot * 128:ki * 512 + (ot + 1) * 128],
                            lnb[:, ki * SC:(ki + 1) * SC], ki == 0,
                            ki == NT - 1, inc=(ki == NT - 1))
                    rec.wait("scalar", PE, C["pe"])
                    rec.wait("scalar", VE, C["ve"])
                    a6 = aop("activation", sg, bg[:], AF.Silu)
                    rec.wait("vector", AE, a6)
                    pj = (4 * i + ot) % 2
                    rec.wait("vector", QS, fst_free[pj])
                    v_h = vop("tensor_tensor", fstg[pj], sg, bu[:], ALU.mult)
                    rec.wait("sync", VE, v_h)
                    fst_free[pj] = dmas(
                        fh_dr[ob * 512 + ot * 128:ob * 512 + (ot + 1) * 128,
                              i * SC:(i + 1) * SC], fstg[pj])
                wld[ob] = C["pe"]
        barrier()

    if "p3e" in phases:
        cv = Carver()
        NHT = HFF // 128
        fhb = cv.bf16(NHT * SC)
        x1c = cv.f32(NT * SC)
        w2_sb = [cv.bf16(NHT * 128), cv.bf16(NHT * 128)]
        x2s = [cv.f32(SC), cv.f32(SC)]
        wfree = [0, 0]
        st_free = [0, 0]
        for i in range(NOC):
            rec.wait("sync", VE, C["ve"])
            rec.wait("sync", PE, C["pe"])
            fld = dmas(**rl(fhb, fh_dr[:, i * SC:(i + 1) * SC], SC))
            x1ld = dmas(**rl(x1c, x1_dr[:, i * SC:(i + 1) * SC], SC))
            for ot in range(NT):
                p = ot % 2
                rec.wait("gpsimd", PE, wfree[p])
                lw = dmag(**rl(w2_sb[p], w2p_dr[:, ot * 128:(ot + 1) * 128],
                               128))
                bi = pb[ot % 4]
                rec.wait("tensor", QS, fld)
                rec.wait("tensor", QG, lw)
                rec.wait("tensor", VE, C["ve"])
                for ki in range(NHT):
                    mmi(bi[:], w2_sb[p][:, ki * 128:(ki + 1) * 128],
                        fhb[:, ki * SC:(ki + 1) * SC], ki == 0, ki == NHT - 1,
                        inc=(ki == NHT - 1))
                wfree[p] = C["pe"]
                rec.wait("vector", PE, C["pe"])
                rec.wait("vector", QS, x1ld)
                pj = ot % 2
                rec.wait("vector", QS, st_free[pj])
                v_x2 = vop("tensor_tensor", x2s[pj],
                           x1c[:, ot * SC:(ot + 1) * SC], bi[:], ALU.add)
                rec.wait("sync", VE, v_x2)
                st_free[pj] = dmas(
                    x2o[ot * 128:(ot + 1) * 128, i * SC:(i + 1) * SC],
                    x2s[pj])

    for name, (out_ap, src_ap) in dbg.items():
        dmas(out_ap, src_ap)

    rec.wait("sync", QS, C["qs"])
    rec.wait("gpsimd", QG, C["qg"])

    with nc.Block() as block:
        @block.sync
        def _(e):
            rec.play("sync_pre", e)
            for h in nc.Switch(engines=[e], index=[RT["s_hv"]], n=2):
                RT["h_arm"] = h
                rec.play("sync", e)

        @block.tensor
        def _(e):
            rec.play("tensor", e)

        @block.vector
        def _(e):
            rec.play("vector", e)

        @block.scalar
        def _(e):
            rec.play("scalar", e)

        @block.gpsimd
        def _(e):
            rec.play("gpsimd_pre", e)
            for h in nc.Switch(engines=[e], index=[RT["g_hv"]], n=2):
                RT["h_arm_g"] = h
                rec.play("gpsimd", e)

    es.__exit__(None, None, None)
    return nc



def _hadamard(n):
    h = np.array([[1.0]], dtype=np.float32)
    while h.shape[0] < n:
        h = np.block([[h, h], [h, -h]])
    return np.ascontiguousarray(h.astype(np.float32))


def kernel(L0_attn_hidden_states, L0_attn_in_proj_w, L0_attn_dt_proj_w,
           L0_attn_dt_proj_b, L0_attn_A_log, L0_attn_D_param,
           L0_attn_out_proj_w, L0_attn_gate_w, L0_attn_up_w, L0_attn_down_w,
           L0_attn_norm1_w, L0_attn_norm2_w, L0_ffn_W1, L0_ffn_W2, L0_ffn_W3):
    x0 = np.asarray(L0_attn_hidden_states, dtype=np.float32)
    ipT = np.ascontiguousarray(np.asarray(L0_attn_in_proj_w, np.float32).T)
    dtT = np.ascontiguousarray(np.asarray(L0_attn_dt_proj_w, np.float32).T)
    opT = np.ascontiguousarray(np.asarray(L0_attn_out_proj_w, np.float32).T)
    gT = np.ascontiguousarray(np.asarray(L0_attn_gate_w, np.float32).T)
    uT = np.ascontiguousarray(np.asarray(L0_attn_up_w, np.float32).T)
    dT = np.ascontiguousarray(np.asarray(L0_attn_down_w, np.float32).T)
    w1T = np.ascontiguousarray(np.asarray(L0_ffn_W1, np.float32).T)
    w3T = np.ascontiguousarray(np.asarray(L0_ffn_W3, np.float32).T)
    w2T = np.ascontiguousarray(np.asarray(L0_ffn_W2, np.float32).T)
    dtb = np.asarray(L0_attn_dt_proj_b, np.float32)
    alog = np.asarray(L0_attn_A_log, np.float32)
    dpar = np.asarray(L0_attn_D_param, np.float32)
    n1 = np.asarray(L0_attn_norm1_w, np.float32)
    n2 = np.asarray(L0_attn_norm2_w, np.float32)

    h128 = _hadamard(128)
    h64 = _hadamard(64)
    onesc = np.ones((128, 1), np.float32)
    rng = np.random.default_rng(12345)
    rndf = rng.integers(1 << 20, 1 << 30, size=(1, 16), dtype=np.int32)

    n1T = np.ascontiguousarray(n1.reshape(16, 128).T)
    n2T = np.ascontiguousarray(n2.reshape(16, 128).T)

    x0bs = [np.ascontiguousarray(x0[b].T) for b in range(4)]
    in_maps = []
    for c in range(8):
        b, h = c // 2, c % 2
        wa_ = np.ascontiguousarray(np.concatenate([
            ipT[:, 2048 + 256 * h:2048 + 256 * h + 256],
            ipT[:, 2560 + 256 * h:2560 + 256 * h + 256],
            ipT[:, 3072 + 1024 * h:3072 + 1024 * h + 1024],
            ipT[:, 5120:5248]], axis=1))
        in_maps.append(dict(
            x0b=x0bs[b], wa=wa_,
            wz=np.ascontiguousarray(ipT[:, 0:2048]),
            wdtp=np.ascontiguousarray(dtT[:, 1024 * h:1024 * (h + 1)]),
            wop=opT, wgate=gT, wupp=uT, wdown=dT, w1t=w1T, w3t=w3T, w2t=w2T,
            h128=h128, h64=h64,
            alog=np.ascontiguousarray(alog[1024 * h:1024 * (h + 1)]),
            dtbT=np.ascontiguousarray(
                dtb[1024 * h:1024 * (h + 1)].reshape(8, 128).T),
            dparT=np.ascontiguousarray(
                dpar[1024 * h:1024 * (h + 1)].reshape(8, 128).T),
            n1wT=n1T, n2wT=n2T, onesc=onesc, rndf=rndf))

    nc = build_nc(PHASES, DEBUG)
    res = run_bass_kernel_spmd(nc, in_maps, core_ids=list(range(8)))

    out = np.empty((4, L, D), np.float32)
    for c in range(8):
        b, h = c // 2, c % 2
        x2 = res.results[c]["x2o"]
        for i in range(NOC):
            gi = 2 * i + h
            out[b, gi * SC:(gi + 1) * SC, :] = x2[:, i * SC:(i + 1) * SC].T
    globals()["_last_results"] = res
    return out


# version 03
# speedup vs baseline: 1.3971x; 1.3971x over previous
import contextlib
import numpy as np
import ml_dtypes
import concourse.bass as bass
import concourse.mybir as mybir
from concourse.bass_utils import run_bass_kernel_spmd

F32 = mybir.dt.float32
BF16 = mybir.dt.bfloat16
I32 = mybir.dt.int32
AF = mybir.ActivationFunctionType
ALU = mybir.AluOpType
BF = ml_dtypes.bfloat16

D = 2048
L = 8192
DIN = 2048
NST = 32
DTRK = 128
DFF = 5632
HFF = 8192
EPS = 1e-5

NT = 16
TC = 512
NCK = L // TC
TSC = 2048
NSC = L // TSC
SC = 512
NOC = 4096 // SC
NJT = 8
WA_OF = 1664
WA_OT = 13

DEBUG = ()
PHASES = ("fold", "p01", "p2", "p3a", "p3b", "p3c", "p3d", "p3e")

ENGS = ("sync", "tensor", "vector", "scalar", "gpsimd")


class Rec:
    def __init__(self):
        self.items = {e: [] for e in ENGS}
        self.items["sync_pre"] = []
        self.items["gpsimd_pre"] = []

    def call(self, eng, fn, *a, inc=None, **kw):
        self.items[eng].append((fn, a, kw, inc))

    def raw(self, eng, f):
        self.items[eng].append((None, f, None, None))

    def wait(self, eng, sem, val):
        if val > 0:
            self.call(eng, "wait_ge", sem, val)

    def play(self, name, eng):
        for fn, a, kw, inc in self.items[name]:
            if fn is None:
                a(eng)
            else:
                r = getattr(eng, fn)(*a, **kw)
                if inc is not None:
                    r.then_inc(*inc)


def rl(dst_ap, src_ap, width):
    return dict(out=dst_ap.rearrange("p (i w) -> p i w", w=width),
                in_=src_ap.rearrange("(i p) w -> p i w", p=128))


def build_nc(phases=PHASES, debug=()):
    nc = bass.Bass("TRN2", target_bir_lowering=False, debug=False)

    def din(name, shape, dt=F32):
        return nc.dram_tensor(name, list(shape), dt, kind="ExternalInput").ap()

    x0b = din("x0b", (D, L))
    wa = din("wa", (D, WA_OF))
    wz = din("wz", (D, D))
    wdtp = din("wdtp", (DTRK, 1024))
    wop = din("wop", (DIN, D))
    wgate = din("wgate", (D, DFF))
    wupp = din("wupp", (D, DFF))
    wdown = din("wdown", (DFF, D))
    w1t = din("w1t", (D, HFF))
    w3t = din("w3t", (D, HFF))
    w2t = din("w2t", (HFF, D))
    h128 = din("h128", (128, 128))
    h64 = din("h64", (64, 64))
    alog = din("alog", (1024, NST))
    dtbT = din("dtbT", (128, 8))
    dparT = din("dparT", (128, 8))
    n1wT = din("n1wT", (128, 16))
    n2wT = din("n2wT", (128, 16))
    onesc = din("onesc", (128, 1))
    rndf = din("rndf", (1, 16), I32)

    x2o = nc.dram_tensor("x2o", [D, 4096], F32, kind="ExternalOutput").ap()

    def dt_(name, shape, dt=F32, shared=False):
        return nc.dram_tensor(name, list(shape), dt, kind="Internal",
                              addr_space="Shared" if shared else "Local").ap()

    x_dr = dt_("x_dr", (256, L))
    b_dr = dt_("b_dr", (256, L))
    c_dr = dt_("c_dr", (1024, L))
    dt_dr = dt_("dt_dr", (1024, L))
    sz_dr = dt_("sz_dr", (D, L), BF16)
    mr_dr = dt_("mr_dr", (2, L))
    row_dr = dt_("row_dr", (4, L))
    xn_dr = dt_("xn_dr", (D, 4096), BF16)
    hx_dr = dt_("hx_dr", (D, 4096))
    x1_dr = dt_("x1_dr", (D, 4096))
    ln2_dr = dt_("ln2_dr", (D, 4096), BF16)
    mh_dr = dt_("mh_dr", (DFF, 4096), BF16)
    fh_dr = dt_("fh_dr", (HFF, 4096), BF16)
    m1_dr = dt_("m1_dr", (HFF, D), BF16)
    w2p_dr = dt_("w2p_dr", (HFF, D), BF16)
    yd_sh = dt_("yd_sh", (2, 1024, L), shared=True)
    fl_sh = dt_("fl_sh", (2, 8), I32, shared=True)

    dbg = {}
    for name, src in (("dt", dt_dr), ("bc", b_dr), ("cc", c_dr), ("xs", x_dr),
                      ("sz", sz_dr), ("hx", hx_dr), ("x1", x1_dr),
                      ("ln2", ln2_dr), ("mh", mh_dr), ("fh", fh_dr),
                      ("w2p", w2p_dr), ("xn", xn_dr), ("mr", mr_dr),
                      ("yd", yd_sh)):
        if name in debug:
            dbg[name] = (nc.dram_tensor("dbg_" + name, list(src.shape),
                                        src.dtype, kind="ExternalOutput").ap(),
                         src)

    es = contextlib.ExitStack()
    es.__enter__()

    def sbt(name, p, w, dt=F32):
        return es.enter_context(nc.sbuf_tensor(name, [p, w], dt))

    a_sb = sbt("a_sb", 128, NJT * NST)
    dtb_sb = sbt("dtb_sb", 128, 8)
    dpar_sb = sbt("dpar_sb", 128, 8)
    n1w_sb = sbt("n1w_sb", 128, 16)
    n2w_sb = sbt("n2w_sb", 128, 16)
    ones_sb = sbt("ones_sb", 128, 1, BF16)
    h128_sb = sbt("h128_sb", 128, 128, BF16)
    h64_sb = sbt("h64_sb", 64, 64, BF16)
    carry = sbt("carry", 128, 256)
    rnd_sb = sbt("rnd_sb", 1, 16, I32)
    cb_sb = sbt("cb_sb", 128, 2)
    pf_sb = sbt("pf_sb", 1, 1, I32)
    mrow = sbt("mrow", 1, TC)
    msrow = sbt("msrow", 1, TC)
    vrow = sbt("vrow", 1, TC)
    rrow = sbt("rrow", 1, TC)

    POOLW = 48500
    pool = sbt("pool", 128, POOLW)

    class Carver:
        def __init__(self):
            self.off = 0

        def f32(self, w):
            assert self.off + w <= POOLW, f"pool overflow {self.off}+{w}"
            ap = pool[:, self.off:self.off + w]
            self.off += w
            return ap

        def bf16(self, w):
            assert w % 2 == 0
            return self.f32(w // 2).bitcast(BF16)

    pb = [es.enter_context(nc.psum_tensor(f"pb{i}", [128, 512], F32))
          for i in range(8)]

    sems = {s: es.enter_context(nc.semaphore(s))
            for s in ("qs", "qg", "pe", "ve", "ae", "po", "bar", "ge",
                      "qb0", "qb1", "qc0", "qc1", "qx", "qd", "qy0", "qy1",
                      "qw0", "qw1", "qxn", "qz0", "qz1", "qt0", "qt1")}
    QS, QG, PE, VE, AE, PO, BAR, GE = (sems[s] for s in
                                       ("qs", "qg", "pe", "ve", "ae", "po",
                                        "bar", "ge"))

    rec = Rec()
    C = {"qs": 0, "qg": 0, "pe": 0, "ve": 0, "ae": 0, "bar": 0, "ge": 0}
    RT = {}

    _worig = rec.wait

    def _wait_fix(eng, sem, val):
        if sem is QS:
            val = C["qs"]
        elif sem is QG:
            val = C["qg"]
        _worig(eng, sem, val)
    rec.wait = _wait_fix

    def dmas(out, in_, **kw):
        _worig("sync", QS, C["qs"])
        C["qs"] += 16
        rec.call("sync", "dma_start", out=out, in_=in_, inc=(QS, 16), **kw)
        return C["qs"]

    def dmag(out, in_, **kw):
        _worig("gpsimd", QG, C["qg"])
        C["qg"] += 16
        rec.call("gpsimd", "dma_start", out=out, in_=in_, inc=(QG, 16), **kw)
        return C["qg"]

    def dma_st(queue, skey, out, in_, **kw):
        C[skey] = C.get(skey, 0) + 16
        rec.call(queue, "dma_start", out=out, in_=in_,
                 inc=(sems[skey], 16), **kw)
        return C[skey]

    def wait_st(eng, skey, val):
        _worig(eng, sems[skey], val)

    def vop(fn, *a, **kw):
        _worig("vector", VE, C["ve"])
        C["ve"] += 1
        rec.call("vector", fn, *a, inc=(VE, 1), **kw)
        return C["ve"]

    def aop(fn, *a, **kw):
        _worig("scalar", AE, C["ae"])
        C["ae"] += 1
        rec.call("scalar", fn, *a, inc=(AE, 1), **kw)
        return C["ae"]

    def vopn(fn, *a, wv=0, wa=0, wg=0, ws=(), **kw):
        for k, v in ws:
            _worig("vector", sems[k], v)
        if wv:
            _worig("vector", VE, wv)
        if wa:
            _worig("vector", AE, wa)
        if wg:
            _worig("vector", GE, wg)
        C["ve"] += 1
        rec.call("vector", fn, *a, inc=(VE, 1), **kw)
        return C["ve"]

    def gopn(fn, *a, wv=0, wa=0, wg=0, ws=(), **kw):
        for k, v in ws:
            _worig("gpsimd", sems[k], v)
        if wv:
            _worig("gpsimd", VE, wv)
        if wa:
            _worig("gpsimd", AE, wa)
        if wg:
            _worig("gpsimd", GE, wg)
        C["ge"] += 1
        rec.call("gpsimd", fn, *a, inc=(GE, 1), **kw)
        return C["ge"]

    def aopx(fn, *a, wv=0, wg=0, ws=(), **kw):
        for k, v in ws:
            _worig("scalar", sems[k], v)
        if wv:
            _worig("scalar", VE, wv)
        if wg:
            _worig("scalar", GE, wg)
        return aop(fn, *a, **kw)

    def mmi(out, lhsT, rhs, start, stop, inc=False):
        if inc:
            C["pe"] += 1
            rec.call("tensor", "matmul", out, lhsT, rhs, start=start,
                     stop=stop, inc=(PE, 1))
        else:
            rec.call("tensor", "matmul", out, lhsT, rhs, start=start,
                     stop=stop)
        return C["pe"]

    def barrier():
        rec.wait("sync", QS, C["qs"])
        rec.wait("gpsimd", QG, C["qg"])
        C["bar"] += len(ENGS)
        tgt = C["bar"]
        for e in ENGS:
            rec.call(e, "sem_inc", BAR, 1)
            rec.wait(e, BAR, tgt)

    def pre_sync(e):
        pid = e.partition_id()
        hreg = e.alloc_register("h_sp")
        e.reg_alu(hreg, pid, 2, ALU.mod)
        RT["s_hv"] = bass.RuntimeValue(hreg, min_val=0, max_val=1)

    def pre_gp(e):
        pid = e.partition_id()
        hreg = e.alloc_register("h_gp")
        e.reg_alu(hreg, pid, 2, ALU.mod)
        RT["g_hv"] = bass.RuntimeValue(hreg, min_val=0, max_val=1)
        RT["g_cnt"] = e.alloc_register("pollcnt")
        RT["g_fv"] = e.alloc_register("pollfv")
        RT["g_want"] = e.alloc_register("pollwant")
        e.reg_mov(RT["g_cnt"], 0)

    rec.raw("sync_pre", pre_sync)
    rec.raw("gpsimd_pre", pre_gp)

    def dyn_dma_s(tensor_ap, dims, offs_fn, other, is_store):
        C["qs"] += 16

        wait_prev = C["qs"] - 16

        def run(e, tensor_ap=tensor_ap, dims=dims, offs_fn=offs_fn,
                other=other, is_store=is_store, wait_prev=wait_prev):
            if wait_prev > 0:
                e.wait_ge(QS, wait_prev)
            ap = bass.AP(tensor_ap.tensor, offs_fn(RT["h_arm"]), dims)
            if is_store:
                e.dma_start(out=ap, in_=other).then_inc(QS, 16)
            else:
                e.dma_start(out=other, in_=ap).then_inc(QS, 16)
        rec.raw("sync", run)
        return C["qs"]

    dmag(ones_sb[:], onesc)
    dmag(h128_sb[:], h128)
    dmag(h64_sb[:], h64)
    dmas(dtb_sb[:], dtbT)
    dmas(dpar_sb[:], dparT)
    dmas(n1w_sb[:], n1wT)
    dmas(n2w_sb[:], n2wT)
    dmas(rnd_sb[:], rndf)
    cvc = Carver()
    atmp = cvc.f32(NJT * NST)
    ld = dmas(**rl(atmp, alog, NST))
    rec.wait("scalar", QS, ld)
    aop("activation", a_sb[:], atmp, AF.Exp)
    rec.wait("vector", AE, C["ae"])
    vop("tensor_scalar_mul", a_sb[:], a_sb[:], -1.0)
    vop("memset", carry[:], 0.0)
    vop("memset", cb_sb[:, 0:1], EPS)
    vop("memset", cb_sb[:, 1:2], 1.0)
    nc.const_aps.register(F32, EPS, cb_sb[:, 0:1])
    nc.const_aps.register(F32, 1.0, cb_sb[:, 1:2])
    barrier()

    if "fold" in phases:
        cv = Carver()
        t1b = [cv.bf16(2048), cv.bf16(2048)]
        t2b = [cv.bf16(2048), cv.bf16(2048)]
        stg = [cv.bf16(512) for _ in range(4)]
        stg2 = [cv.bf16(512) for _ in range(4)]
        s1sem = ("qw0", "qw1", "qz0", "qz1")
        s2sem = ("qt0", "qt1", "qy0", "qy1")

        t1_rd = [0, 0]
        stg_st = [C.get(s, 0) for s in s1sem]
        bank_ae = [0, 0, 0, 0]
        ns = 0
        for hl in range(64):
            p = hl % 2
            w2src = bass.AP(w2t.tensor, hl * 2048, [[64 * 2048, 128], [1, 2048]])
            rec.wait("gpsimd", PE, t1_rd[p])
            lda = dma_st("gpsimd", "qd", t1b[p], w2src)
            for dc in range(4):
                b = ns % 4
                bank = pb[b]
                rec.wait("tensor", AE, bank_ae[b])
                wait_st("tensor", "qd", lda)
                mmi(bank[:], h128_sb[:], t1b[p][:, dc * 512:(dc + 1) * 512],
                    True, True, inc=True)
                rec.wait("scalar", PE, C["pe"])
                bank_ae[b] = aopx("copy", stg[b], bank[:],
                                  ws=((s1sem[b], stg_st[b]),))
                m1dst = bass.AP(m1_dr.tensor, hl * 2048 + dc * 512,
                                [[64 * 2048, 128], [1, 512]])
                rec.wait("sync", AE, bank_ae[b])
                stg_st[b] = dma_st("sync", s1sem[b], m1dst, stg[b])
                ns += 1
            t1_rd[p] = C["pe"]
        t2_rd = [0, 0]
        stg2_st = [C.get(s, 0) for s in s2sem]
        bank2_ae = [0, 0, 0, 0]
        ns = 0
        for hh in range(128):
            p = hh % 2
            m1src = bass.AP(m1_dr.tensor, hh * 64 * 2048,
                            [[2048, 64], [1, 2048]])
            rec.wait("gpsimd", PE, t2_rd[p])
            if hh == 0:
                for i in range(4):
                    wait_st("gpsimd", s1sem[i], stg_st[i])
            lda = dma_st("gpsimd", "qx", t2b[p][0:64, :], m1src)
            for dc in range(4):
                b = ns % 4
                bank = pb[4 + b]
                rec.wait("tensor", AE, bank2_ae[b])
                wait_st("tensor", "qx", lda)
                mmi(bank[0:64, :],
                    h64_sb[:], t2b[p][0:64, dc * 512:(dc + 1) * 512],
                    True, True, inc=True)
                rec.wait("scalar", PE, C["pe"])
                bank2_ae[b] = aopx("copy", stg2[b][0:64, :], bank[0:64, :],
                                   ws=((s2sem[b], stg2_st[b]),))
                w2pdst = bass.AP(w2p_dr.tensor, hh * 64 * 2048 + dc * 512,
                                 [[2048, 64], [1, 512]])
                rec.wait("sync", AE, bank2_ae[b])
                stg2_st[b] = dma_st("sync", s2sem[b], w2pdst,
                                    stg2[b][0:64, :])
                ns += 1
            t2_rd[p] = C["pe"]
        for i in range(4):
            wait_st("sync", s2sem[i], stg2_st[i])
        barrier()

    if "p01" in phases:
        cv = Carver()
        wa_sb = cv.bf16(NT * WA_OF)
        wz_sb = cv.bf16(NT * 2048)
        wdt_sb = cv.bf16(1024)
        x0bf = cv.f32(NT * TC)
        ubuf = cv.bf16(NT * TC)
        xb2 = [cv.bf16(TC), cv.bf16(TC)]
        dtrb = cv.bf16(TC)
        stg = [cv.f32(TC), cv.f32(TC), cv.f32(TC)]
        m_b = cv.f32(TC)
        r_b = cv.f32(TC)
        rr_b = cv.f32(TC)
        e1 = cv.f32(TC)
        e2 = cv.f32(TC)

        dmag(**rl(wa_sb, wa, WA_OF))
        dmag(**rl(wz_sb, wz, 2048))
        ldw_done = dmag(wdt_sb, wdtp)

        def wa_st(ki, ot):
            return wa_sb[:, ki * WA_OF + ot * 128:ki * WA_OF + (ot + 1) * 128]

        def wz_st(ki, ot):
            return wz_sb[:, ki * 2048 + ot * 128:ki * 2048 + (ot + 1) * 128]

        slot_dma = [0, 0, 0]
        bank_rel = [0, 0, 0, 0]
        bank_rel_ve = [0, 0, 0, 0]
        for k in range(NCK):
            t0 = k * TC
            rec.wait("sync", VE, C["ve"])
            rec.wait("sync", PE, C["pe"])
            xld = dmas(**rl(x0bf, x0b[:, t0:t0 + TC], TC))
            for ti in range(NT):
                rec.wait("scalar", QS, xld)
                rec.wait("scalar", PE, C["pe"])
                aop("copy", xb2[0], x0bf[:, ti * TC:(ti + 1) * TC])
                a2 = aop("activation", xb2[1], x0bf[:, ti * TC:(ti + 1) * TC],
                         AF.Square)
                rec.wait("tensor", AE, a2)
                if ti == 0:
                    rec.wait("tensor", VE, C["ve"])
                mmi(pb[6][0:1, :], ones_sb[:], xb2[0], ti == 0, ti == NT - 1)
                mmi(pb[7][0:1, :], ones_sb[:], xb2[1], ti == 0, ti == NT - 1,
                    inc=True)
            rec.wait("vector", PE, C["pe"])
            vop("tensor_scalar_mul", mrow[:], pb[6][0:1, :], 1.0 / D)
            vop("tensor_scalar_mul", msrow[:], pb[7][0:1, :], 1.0 / D)
            vop("tensor_tensor", vrow[:], mrow[:], mrow[:], ALU.mult)
            v_var = vop("tensor_tensor", vrow[:], msrow[:], vrow[:],
                        ALU.subtract)
            rec.wait("scalar", VE, v_var)
            aop("activation", rrow[:], vrow[:], AF.Ln, bias=EPS)
            a_r = aop("activation", rrow[:], rrow[:], AF.Exp, scale=-0.5)
            rec.wait("sync", VE, v_var)
            rec.wait("sync", AE, a_r)
            dmas(mr_dr[0:1, t0:t0 + TC], mrow[:])
            dmas(mr_dr[1:2, t0:t0 + TC], rrow[:])
            rec.wait("sync", QS, C["qs"])
            dmas(m_b, mr_dr[0:1, t0:t0 + TC].partition_broadcast(128))
            bc1 = dmas(r_b, mr_dr[1:2, t0:t0 + TC].partition_broadcast(128))
            for ti in range(NT):
                rec.wait("vector", QS, bc1)
                sl = x0bf[:, ti * TC:(ti + 1) * TC]
                vop("tensor_tensor", sl, sl, m_b, ALU.subtract)
                vln = vop("tensor_tensor", sl, sl, r_b, ALU.mult)
                rec.wait("scalar", VE, vln)
                rec.wait("scalar", PE, C["pe"])
                a3 = aop("activation", xb2[ti % 2], sl, AF.Square)
                rec.wait("tensor", AE, a3)
                if ti == 0:
                    rec.wait("tensor", VE, C["ve"])
                mmi(pb[5][0:1, :], ones_sb[:], xb2[ti % 2], ti == 0,
                    ti == NT - 1, inc=True)
            rec.wait("vector", PE, C["pe"])
            v_s2 = vop("tensor_scalar_mul", vrow[:], pb[5][0:1, :], 1.0 / D)
            rec.wait("scalar", VE, v_s2)
            aop("activation", rrow[:], vrow[:], AF.Ln, bias=EPS)
            a_rr = aop("activation", rrow[:], rrow[:], AF.Exp, scale=-0.5)
            rec.wait("sync", AE, a_rr)
            dmas(row_dr[0:1, t0:t0 + TC], rrow[:])
            rec.wait("sync", QS, C["qs"])
            bc2 = dmas(rr_b, row_dr[0:1, t0:t0 + TC].partition_broadcast(128))
            rec.wait("vector", PE, C["pe"])
            for ti in range(NT):
                rec.wait("vector", QS, bc2)
                u_done = vop("scalar_tensor_tensor",
                             ubuf[:, ti * TC:(ti + 1) * TC],
                             x0bf[:, ti * TC:(ti + 1) * TC],
                             n1w_sb[:, ti:ti + 1], rr_b, ALU.mult, ALU.mult)
            rec.wait("tensor", VE, u_done)
            if k == 0:
                rec.wait("tensor", QG, ldw_done)

            def u_mv(ki):
                return ubuf[:, ki * TC:(ki + 1) * TC]

            state = {"g": 0}

            def ogroup(stf, nof, evac):
                for ot in range(nof):
                    bi = state["g"] % 4
                    rec.wait("tensor", AE, bank_rel[bi])
                    rec.wait("tensor", VE, bank_rel_ve[bi])
                    for ki in range(NT):
                        mmi(pb[bi][:], stf(ki, ot), u_mv(ki), ki == 0,
                            ki == NT - 1, inc=(ki == NT - 1))
                    evac(pb[bi], ot, bi)
                    state["g"] += 1

            def evac_wa(bank, ot, bi):
                rec.wait("scalar", PE, C["pe"])
                if ot == 12:
                    bank_rel[bi] = aop("copy", dtrb, bank[:])
                else:
                    si = ot % 3
                    rec.wait("scalar", QS, slot_dma[si])
                    a = aop("copy", stg[si], bank[:])
                    if ot < 2:
                        dst = x_dr[ot * 128:(ot + 1) * 128, t0:t0 + TC]
                    elif ot < 4:
                        dst = b_dr[(ot - 2) * 128:(ot - 1) * 128, t0:t0 + TC]
                    else:
                        dst = c_dr[(ot - 4) * 128:(ot - 3) * 128, t0:t0 + TC]
                    rec.wait("sync", AE, a)
                    slot_dma[si] = dmas(dst, stg[si])
                    bank_rel[bi] = a

            def evac_wz(bank, ot, bi):
                rec.wait("scalar", PE, C["pe"])
                si = ot % 3
                rec.wait("scalar", QS, slot_dma[si])
                s = stg[si].bitcast(BF16)[:, 0:TC]
                a = aop("activation", s, bank[:], AF.Silu)
                rec.wait("sync", AE, a)
                slot_dma[si] = dmas(sz_dr[ot * 128:(ot + 1) * 128, t0:t0 + TC],
                                    s)
                bank_rel[bi] = a

            ogroup(wa_st, WA_OT, evac_wa)
            ogroup(wz_st, NT, evac_wz)
            rec.wait("tensor", AE, C["ae"])
            for ot in range(8):
                bi = state["g"] % 4
                rec.wait("tensor", AE, bank_rel[bi])
                rec.wait("tensor", VE, bank_rel_ve[bi])
                mmi(pb[bi][:], wdt_sb[:, ot * 128:(ot + 1) * 128], dtrb,
                    True, True, inc=True)
                state["g"] += 1
                rec.wait("vector", PE, C["pe"])
                rec.wait("vector", AE, C["ae"])
                v1 = vop("tensor_scalar_add", e1, pb[bi][:],
                         dtb_sb[:, ot:ot + 1])
                bank_rel_ve[bi] = v1
                rec.wait("scalar", VE, v1)
                aop("activation", e2, e1, AF.Abs)
                aop("activation", e2, e2, AF.Exp, scale=-1.0)
                a4 = aop("activation", e2, e2, AF.Ln, bias=1.0)
                rec.wait("vector", AE, a4)
                si = ot % 3
                rec.wait("vector", QS, slot_dma[si])
                v2 = vop("scalar_tensor_tensor", stg[si], e1, 0.0, e2,
                         ALU.max, ALU.add)
                rec.wait("sync", VE, v2)
                slot_dma[si] = dmas(
                    dt_dr[ot * 128:(ot + 1) * 128, t0:t0 + TC], stg[si])
        barrier()

    if "p2" in phases:
        cv = Carver()
        dtb = [cv.f32(TSC), cv.f32(TSC)]
        xb = [cv.f32(TSC), cv.f32(TSC)]
        vslb = [cv.f32(TSC), cv.f32(TSC)]
        dab = [cv.f32(TSC), cv.f32(TSC)]
        ubb = [cv.f32(TSC), cv.f32(TSC)]
        hbb = [cv.f32(TSC), cv.f32(TSC)]
        wbb = [cv.f32(TSC), cv.f32(TSC)]
        ybb = [cv.f32(TSC), cv.f32(TSC)]
        Bb = [cv.f32(TSC), cv.f32(TSC)]
        Cb = [cv.f32(TSC), cv.f32(TSC)]
        ystb = [cv.f32(TSC), cv.f32(TSC)]

        NM = NSC * NJT
        ld_dt = [0, 0]
        ld_x = [0, 0]
        rd_dt_ae = [0, 0]
        rd_dt_ve = [0, 0]
        rd_x_ve = [0, 0]
        rd_vsl_ge = [0, 0]
        exp_ae = [0, 0]
        cc_ae = [0, 0]
        scan_ve = [0, 0]
        ub_ge = [0, 0]
        wb_rel = [("v", 0), ("v", 0)]
        acc_ge = [0, 0]
        bld = [0, 0]
        cld = [0, 0]
        yst_st = [C.get("qy0", 0), C.get("qy1", 0)]
        ysum_ve = 0
        carry_w = [0] * (NJT * NST)

        def ld_B(n, j, ts):
            s = n % 2
            rec.wait("sync", GE, ub_ge[s])
            bsrc = bass.AP(b_dr.tensor, (j * 32 + n) * L + ts,
                           [[0, 128], [1, TSC]])
            bld[s] = dma_st("sync", "qb%d" % s, Bb[s], bsrc)

        def ld_C(n, j, ts):
            s = n % 2
            kind, cnt = wb_rel[s]
            rec.wait("sync", VE if kind == "v" else GE, cnt)
            csrc = bass.AP(c_dr.tensor, (j * 128 + n) * L + ts,
                           [[32 * L, 4], [0, 32], [1, TSC]])
            cld[s] = dma_st("sync", "qc%d" % s, Cb[s], csrc)

        def ld_dtx(m):
            slot = m % 2
            S, j = divmod(m, NJT)
            ts = S * TSC
            rec.wait("sync", AE, rd_dt_ae[slot])
            rec.wait("sync", VE, rd_dt_ve[slot])
            dsrc = bass.AP(dt_dr.tensor, j * 128 * L + ts,
                           [[L, 128], [1, TSC]])
            ld_dt[slot] = dma_st("sync", "qd", dtb[slot], dsrc)
            rec.wait("sync", VE, rd_x_ve[slot])
            xsrc = bass.AP(x_dr.tensor, (j * 32) * L + ts,
                           [[0, 4], [L, 32], [1, TSC]])
            ld_x[slot] = dma_st("sync", "qx", xb[slot], xsrc)

        ld_dtx(0)
        for m in range(NM):
            S, j = divmod(m, NJT)
            ts = S * TSC
            slot = m % 2
            v_vsl = vopn("tensor_tensor", vslb[slot], dtb[slot], xb[slot],
                         ALU.mult,
                         ws=(("qd", ld_dt[slot]), ("qx", ld_x[slot])),
                         wg=rd_vsl_ge[slot])
            ld_B(0, j, ts)
            ld_C(0, j, ts)
            ld_B(1, j, ts)
            ld_C(1, j, ts)
            for t in range(NST + 3):
                par = t % 2
                if t < NST:
                    idx = j * NST + t
                    exp_ae[par] = aopx("activation", dab[par], dtb[slot],
                                       AF.Exp, scale=a_sb[:, idx:idx + 1],
                                       wv=scan_ve[par],
                                       ws=(("qd", ld_dt[slot]),))
                    ub_ge[par] = gopn("tensor_tensor", ubb[par], vslb[slot],
                                      Bb[par], ALU.mult,
                                      ws=(("qb%d" % par, bld[par]),),
                                      wv=max(scan_ve[par], v_vsl))
                n = t - 1
                if 0 <= n < NST:
                    pn = n % 2
                    idx = j * NST + n
                    wk, wc = wb_rel[pn]
                    sv = vopn("tensor_tensor_scan", hbb[pn], dab[pn],
                              ubb[pn], carry[:, idx:idx + 1],
                              ALU.mult, ALU.add,
                              wa=max(exp_ae[pn], cc_ae[pn], carry_w[idx]),
                              wg=max(ub_ge[pn], wc if wk == "g" else 0),
                              wv=wc if wk == "v" else 0)
                    scan_ve[pn] = sv
                    if S < NSC - 1:
                        cc_ae[pn] = aopx("copy", carry[:, idx:idx + 1],
                                         hbb[pn][:, TSC - 1:TSC], wv=sv)
                        carry_w[idx] = cc_ae[pn]
                if t + 2 < NST:
                    ld_B(t + 2, j, ts)
                n = t - 2
                if 0 <= n < NST:
                    pn = n % 2
                    if n % 2 == 0:
                        w = vopn("tensor_tensor", wbb[pn], hbb[pn], Cb[pn],
                                 ALU.mult, ws=(("qc%d" % pn, cld[pn]),),
                                 wv=scan_ve[pn], wg=acc_ge[pn])
                        wb_rel[pn] = ("v", w)
                    else:
                        w = gopn("tensor_tensor", wbb[pn], hbb[pn], Cb[pn],
                                 ALU.mult, ws=(("qc%d" % pn, cld[pn]),),
                                 wv=scan_ve[pn], wg=acc_ge[pn])
                        wb_rel[pn] = ("g", w)
                if 2 <= t < NST:
                    ld_C(t, j, ts)
                n = t - 3
                if 0 <= n < NST:
                    pn = n % 2
                    wk, wc = wb_rel[pn]
                    if n < 2:
                        acc_ge[pn] = gopn(
                            "tensor_copy", ybb[pn], wbb[pn],
                            wv=max(wc if wk == "v" else 0, ysum_ve),
                            wg=wc if wk == "g" else 0)
                    else:
                        acc_ge[pn] = gopn(
                            "tensor_tensor", ybb[pn], ybb[pn], wbb[pn],
                            ALU.add, wv=wc if wk == "v" else 0,
                            wg=max(wc if wk == "g" else 0, acc_ge[pn]))
            if m + 1 < NM:
                ld_dtx(m + 1)
            v_ysum = vopn("tensor_tensor", ystb[slot], ybb[0], ybb[1],
                          ALU.add, wg=max(acc_ge[0], acc_ge[1]),
                          ws=(("qy%d" % slot, yst_st[slot]),))
            ysum_ve = v_ysum
            v_yst = vopn("scalar_tensor_tensor", ystb[slot], xb[slot],
                         dpar_sb[:, j:j + 1], ystb[slot], ALU.mult, ALU.add,
                         wv=v_ysum)
            rd_x_ve[slot] = v_yst
            rd_dt_ae[slot] = exp_ae[1]
            rd_dt_ve[slot] = v_vsl
            rd_vsl_ge[slot] = ub_ge[1]
            rec.wait("sync", VE, v_yst)
            C["qy%d" % slot] += 16
            yst_st[slot] = C["qy%d" % slot]

            def store_yd(e, j=j, ts=ts, slot=slot):
                ap = bass.AP(yd_sh.tensor,
                             RT["h_arm"] * 1024 * L + j * 128 * L + ts,
                             [[L, 128], [1, TSC]])
                e.dma_start(out=ap, in_=ystb[slot]).then_inc(
                    sems["qy%d" % slot], 16)
            rec.raw("sync", store_yd)
            if j == NJT - 1:
                wait_st("sync", "qy0", C.get("qy0", 0))
                wait_st("sync", "qy1", C.get("qy1", 0))
                for fi in (2 * S, 2 * S + 1):
                    dyn_dma_s(fl_sh, [[1, 1], [1, 1]],
                              lambda h, fi=fi: h * 8 + fi,
                              rnd_sb[0:1, fi:fi + 1], True)
        barrier()

    if "p3a" in phases:
        cv = Carver()
        wop_sb = cv.bf16(NT * 2048)
        ydt = [cv.f32(SC), cv.f32(SC)]
        y2b = cv.bf16(NT * SC)
        x0c = cv.f32(NT * SC)
        hbuf = cv.f32(NT * SC)
        ztile = [cv.bf16(SC), cv.bf16(SC)]
        hsq = [cv.bf16(SC), cv.bf16(SC)]
        ttile = cv.f32(SC)
        hxs = [cv.f32(SC), cv.f32(SC)]
        xns = [cv.bf16(SC), cv.bf16(SC)]
        m_b = cv.f32(SC)
        r_b = cv.f32(SC)
        rr_b = cv.f32(SC)

        lwop = dmag(**rl(wop_sb, wop, 2048))
        bank_free = [0, 0, 0, 0]
        hx_dma = [0, 0]
        xn_dma = [0, 0]
        for i in range(NOC):
            def poll(e, i=i):
                cnt = RT["g_cnt"]
                fv = RT["g_fv"]
                want = RT["g_want"]
                oh = 1 - RT["h_arm_g"]
                e.load(want, rnd_sb[0:1, i:i + 1])
                e.reg_add(cnt, cnt, 16)
                e.dma_start(out=pf_sb[:], in_=fl_sh[oh:oh + 1, i:i + 1]
                            ).then_inc(PO, 16)
                e.wait_ge(PO, cnt)
                e.load(fv, pf_sb[0:1, 0:1])
                with e.While(bass.RuntimeValue(fv) != bass.RuntimeValue(want)):
                    e.reg_add(cnt, cnt, 16)
                    e.dma_start(out=pf_sb[:], in_=fl_sh[oh:oh + 1, i:i + 1]
                                ).then_inc(PO, 16)
                    e.wait_ge(PO, cnt)
                    e.load(fv, pf_sb[0:1, 0:1])
                e.sem_inc(QG, 16)
            C["qg"] += 16
            rec.raw("gpsimd", poll)
            pollc = C["qg"]

            rec.wait("sync", QG, pollc)
            rec.wait("sync", VE, C["ve"])
            rec.wait("sync", PE, C["pe"])

            dyn_dma_s(x0b, [[L, 128], [128 * L, 16], [1, SC]],
                      lambda h, i=i: h * 512 + 1024 * i,
                      x0c.rearrange("p (j t) -> p j t", t=SC), False)
            dyn_dma_s(mr_dr, [[0, 128], [1, SC]],
                      lambda h, i=i: h * 512 + 1024 * i, m_b, False)
            ld_main = dyn_dma_s(mr_dr, [[0, 128], [1, SC]],
                                lambda h, i=i: L + h * 512 + 1024 * i,
                                r_b, False)

            zcons = {}
            for ti in range(NT):
                p = ti % 2
                zw = zcons[ti - 2] if ti >= 2 else 0

                def ld_zy(e, i=i, ti=ti, p=p, zw=zw):
                    if zw:
                        e.wait_ge(VE, zw)
                    ap = bass.AP(sz_dr.tensor,
                                 ti * 128 * L + RT["h_arm"] * 512 + 1024 * i,
                                 [[L, 128], [1, SC]])
                    e.dma_start(out=ztile[p], in_=ap).then_inc(
                        sems["qz%d" % p], 16)
                    s_, jj = (0, ti) if ti < 8 else (1, ti - 8)
                    ap2 = bass.AP(yd_sh.tensor,
                                  s_ * 1024 * L + jj * 128 * L
                                  + RT["h_arm"] * 512 + 1024 * i,
                                  [[L, 128], [1, SC]])
                    e.dma_start(out=ydt[p], in_=ap2).then_inc(
                        sems["qt%d" % p], 16)
                C["qz%d" % p] = C.get("qz%d" % p, 0) + 16
                C["qt%d" % p] = C.get("qt%d" % p, 0) + 16
                rec.raw("sync", ld_zy)
                wait_st("vector", "qz%d" % p, C["qz%d" % p])
                wait_st("vector", "qt%d" % p, C["qt%d" % p])
                if ti == 0:
                    rec.wait("vector", QS, ld_main)
                    rec.wait("vector", PE, C["pe"])
                zcons[ti] = vop("tensor_tensor",
                                y2b[:, ti * SC:(ti + 1) * SC],
                                ydt[p], ztile[p],
                                ALU.mult)
            y2done = zcons[NT - 1]

            rec.wait("tensor", VE, y2done)
            if i == 0:
                rec.wait("tensor", QG, lwop)
            for ot in range(NT):
                bi = ot % 4
                rec.wait("tensor", VE, bank_free[bi])
                for ki in range(NT):
                    mmi(pb[bi][:],
                        wop_sb[:, ki * 2048 + ot * 128:ki * 2048 + (ot + 1) * 128],
                        y2b[:, ki * SC:(ki + 1) * SC], ki == 0, ki == NT - 1,
                        inc=(ki == NT - 1))
                x0sl = x0c[:, ot * SC:(ot + 1) * SC]
                hsl = hbuf[:, ot * SC:(ot + 1) * SC]
                vop("tensor_tensor", ttile, x0sl, m_b, ALU.subtract)
                vop("tensor_tensor", ttile, ttile, r_b, ALU.mult)
                rec.wait("vector", PE, C["pe"])
                bank_free[bi] = vop("tensor_tensor", hsl, ttile, pb[bi][:],
                                    ALU.add)
                pj = ot % 2
                rec.wait("vector", QS, hx_dma[pj])
                v_hx = vop("tensor_tensor", hxs[pj], hsl, x0sl, ALU.add)
                rec.wait("sync", VE, v_hx)
                hx_dma[pj] = dmas(
                    hx_dr[ot * 128:(ot + 1) * 128, i * SC:(i + 1) * SC],
                    hxs[pj])
            for ti in range(NT):
                rec.wait("scalar", PE, C["pe"])
                a5 = aop("activation", hsq[ti % 2],
                         hbuf[:, ti * SC:(ti + 1) * SC], AF.Square)
                rec.wait("tensor", AE, a5)
                if ti == 0:
                    rec.wait("tensor", VE, C["ve"])
                mmi(pb[5][0:1, :], ones_sb[:], hsq[ti % 2], ti == 0,
                    ti == NT - 1, inc=True)
            rec.wait("vector", PE, C["pe"])
            v_s2 = vop("tensor_scalar_mul", vrow[:], pb[5][0:1, :], 1.0 / D)
            rec.wait("scalar", VE, v_s2)
            aop("activation", rrow[:], vrow[:], AF.Ln, bias=EPS)
            a_rr = aop("activation", rrow[:], rrow[:], AF.Exp, scale=-0.5)
            rec.wait("sync", AE, a_rr)
            dmas(row_dr[1:2, i * SC:(i + 1) * SC], rrow[:])
            rec.wait("sync", QS, C["qs"])
            bc3 = dmas(rr_b, row_dr[1:2, i * SC:(i + 1) * SC]
                       .partition_broadcast(128))
            for ti in range(NT):
                rec.wait("vector", QS, bc3)
                pj = ti % 2
                rec.wait("vector", QS, xn_dma[pj])
                v_xn = vop("scalar_tensor_tensor", xns[pj],
                           hbuf[:, ti * SC:(ti + 1) * SC],
                           n2w_sb[:, ti:ti + 1], rr_b, ALU.mult, ALU.mult)
                rec.wait("sync", VE, v_xn)
                xn_dma[pj] = dmas(
                    xn_dr[ti * 128:(ti + 1) * 128, i * SC:(i + 1) * SC],
                    xns[pj])
        barrier()

    if "p3b" in phases:
        cv = Carver()
        wg_sb = [cv.bf16(NT * 512), cv.bf16(NT * 512)]
        wu_sb = [cv.bf16(NT * 512), cv.bf16(NT * 512)]
        xnb = cv.bf16(NT * SC)
        sg = cv.f32(SC)
        mstg = [cv.bf16(SC), cv.bf16(SC)]
        NOB = DFF // 512
        wld = {}
        mst_free = [0, 0]
        for ob in range(NOB):
            p = ob % 2
            if ob >= 2:
                rec.wait("gpsimd", PE, wld[ob - 2])
            dma_st("gpsimd", "qw%d" % p,
                   **rl(wg_sb[p], wgate[:, ob * 512:(ob + 1) * 512], 512))
            lw = dma_st("gpsimd", "qw%d" % p,
                        **rl(wu_sb[p], wupp[:, ob * 512:(ob + 1) * 512], 512))
            for i in range(NOC):
                rec.wait("sync", PE, C["pe"])
                xld = dma_st("sync", "qxn",
                             **rl(xnb, xn_dr[:, i * SC:(i + 1) * SC], SC))
                wait_st("tensor", "qxn", xld)
                wait_st("tensor", "qw%d" % p, lw)
                for ot in range(4):
                    bg = pb[ot]
                    bu = pb[4 + ot]
                    rec.wait("tensor", AE, C["ae"])
                    rec.wait("tensor", VE, C["ve"])
                    for ki in range(NT):
                        mmi(bg[:],
                            wg_sb[p][:, ki * 512 + ot * 128:ki * 512 + (ot + 1) * 128],
                            xnb[:, ki * SC:(ki + 1) * SC], ki == 0,
                            ki == NT - 1)
                    for ki in range(NT):
                        mmi(bu[:],
                            wu_sb[p][:, ki * 512 + ot * 128:ki * 512 + (ot + 1) * 128],
                            xnb[:, ki * SC:(ki + 1) * SC], ki == 0,
                            ki == NT - 1, inc=(ki == NT - 1))
                    rec.wait("scalar", PE, C["pe"])
                    rec.wait("scalar", VE, C["ve"])
                    a6 = aop("activation", sg, bg[:], AF.Silu)
                    rec.wait("vector", AE, a6)
                    pj = (4 * i + ot) % 2
                    rec.wait("vector", QS, mst_free[pj])
                    v_h = vop("tensor_tensor", mstg[pj], sg, bu[:], ALU.mult)
                    rec.wait("sync", VE, v_h)
                    mst_free[pj] = dmas(
                        mh_dr[ob * 512 + ot * 128:ob * 512 + (ot + 1) * 128,
                              i * SC:(i + 1) * SC], mstg[pj])
                wld[ob] = C["pe"]
        barrier()

    if "p3c" in phases:
        cv = Carver()
        NHT = DFF // 128
        hidb = cv.bf16(NHT * SC)
        hxb = cv.f32(NT * SC)
        x1b = cv.f32(NT * SC)
        dw_sb = [cv.bf16(NHT * 128), cv.bf16(NHT * 128)]
        xb2c = [cv.bf16(SC), cv.bf16(SC)]
        l2s = [cv.bf16(SC), cv.bf16(SC)]
        x1s = [cv.f32(SC), cv.f32(SC)]
        m_b = cv.f32(SC)
        r_b = cv.f32(SC)
        wfree = [0, 0]
        l2_dma = [0, 0]
        x1_store = 0
        for i in range(NOC):
            rec.wait("sync", VE, C["ve"])
            rec.wait("sync", PE, C["pe"])
            hld = dmas(**rl(hidb, mh_dr[:, i * SC:(i + 1) * SC], SC))
            hxld = dmas(**rl(hxb, hx_dr[:, i * SC:(i + 1) * SC], SC))
            rec.wait("vector", QS, x1_store)
            for ot in range(NT):
                p = ot % 2
                rec.wait("gpsimd", PE, wfree[p])
                lw = dma_st("gpsimd", "qw%d" % p,
                            **rl(dw_sb[p], wdown[:, ot * 128:(ot + 1) * 128],
                                 128))
                bi = pb[ot % 4]
                rec.wait("tensor", QS, hld)
                wait_st("tensor", "qw%d" % p, lw)
                rec.wait("tensor", VE, C["ve"])
                for ki in range(NHT):
                    mmi(bi[:], dw_sb[p][:, ki * 128:(ki + 1) * 128],
                        hidb[:, ki * SC:(ki + 1) * SC], ki == 0, ki == NHT - 1,
                        inc=(ki == NHT - 1))
                wfree[p] = C["pe"]
                rec.wait("vector", PE, C["pe"])
                rec.wait("vector", QS, hxld)
                vop("tensor_tensor", x1b[:, ot * SC:(ot + 1) * SC],
                    hxb[:, ot * SC:(ot + 1) * SC], bi[:], ALU.add)
            for ti in range(NT):
                rec.wait("scalar", PE, C["pe"])
                sl = x1b[:, ti * SC:(ti + 1) * SC]
                aop("copy", xb2c[0], sl)
                a2 = aop("activation", xb2c[1], sl, AF.Square)
                rec.wait("tensor", AE, a2)
                if ti == 0:
                    rec.wait("tensor", VE, C["ve"])
                mmi(pb[6][0:1, :], ones_sb[:], xb2c[0], ti == 0, ti == NT - 1)
                mmi(pb[7][0:1, :], ones_sb[:], xb2c[1], ti == 0, ti == NT - 1,
                    inc=True)
            rec.wait("vector", PE, C["pe"])
            vop("tensor_scalar_mul", mrow[:], pb[6][0:1, :], 1.0 / D)
            vop("tensor_scalar_mul", msrow[:], pb[7][0:1, :], 1.0 / D)
            vop("tensor_tensor", vrow[:], mrow[:], mrow[:], ALU.mult)
            v_var = vop("tensor_tensor", vrow[:], msrow[:], vrow[:],
                        ALU.subtract)
            rec.wait("scalar", VE, v_var)
            aop("activation", rrow[:], vrow[:], AF.Ln, bias=EPS)
            a_r = aop("activation", rrow[:], rrow[:], AF.Exp, scale=-0.5)
            rec.wait("sync", VE, v_var)
            rec.wait("sync", AE, a_r)
            dmas(row_dr[2:3, i * SC:(i + 1) * SC], mrow[:])
            dmas(row_dr[3:4, i * SC:(i + 1) * SC], rrow[:])
            rec.wait("sync", QS, C["qs"])
            dmas(m_b, row_dr[2:3, i * SC:(i + 1) * SC].partition_broadcast(128))
            bc4 = dmas(r_b, row_dr[3:4, i * SC:(i + 1) * SC]
                       .partition_broadcast(128))
            for ti in range(NT):
                pj = ti % 2
                sl = x1b[:, ti * SC:(ti + 1) * SC]
                rec.wait("vector", QS, bc4)
                rec.wait("vector", QS, l2_dma[pj])
                vop("tensor_tensor", x1s[pj], sl, m_b, ALU.subtract)
                v_l2 = vop("tensor_tensor", l2s[pj], x1s[pj], r_b, ALU.mult)
                rec.wait("sync", VE, v_l2)
                l2_dma[pj] = dmas(
                    ln2_dr[ti * 128:(ti + 1) * 128, i * SC:(i + 1) * SC],
                    l2s[pj])
            x1_store = dmas(x1_dr[:, i * SC:(i + 1) * SC]
                            .rearrange("(j p) t -> p j t", p=128),
                            x1b.rearrange("p (j t) -> p j t", t=SC))
        barrier()

    if "p3d" in phases:
        cv = Carver()
        wg_sb = [cv.bf16(NT * 512), cv.bf16(NT * 512)]
        wu_sb = [cv.bf16(NT * 512), cv.bf16(NT * 512)]
        lnb = cv.bf16(NT * SC)
        sg = cv.f32(SC)
        fstg = [cv.bf16(SC), cv.bf16(SC)]
        NOB = HFF // 512
        wld = {}
        fst_free = [0, 0]
        for ob in range(NOB):
            p = ob % 2
            if ob >= 2:
                rec.wait("gpsimd", PE, wld[ob - 2])
            dma_st("gpsimd", "qw%d" % p,
                   **rl(wg_sb[p], w1t[:, ob * 512:(ob + 1) * 512], 512))
            lw = dma_st("gpsimd", "qw%d" % p,
                        **rl(wu_sb[p], w3t[:, ob * 512:(ob + 1) * 512], 512))
            for i in range(NOC):
                rec.wait("sync", PE, C["pe"])
                xld = dma_st("sync", "qxn",
                             **rl(lnb, ln2_dr[:, i * SC:(i + 1) * SC], SC))
                wait_st("tensor", "qxn", xld)
                wait_st("tensor", "qw%d" % p, lw)
                for ot in range(4):
                    bg = pb[ot]
                    bu = pb[4 + ot]
                    rec.wait("tensor", AE, C["ae"])
                    rec.wait("tensor", VE, C["ve"])
                    for ki in range(NT):
                        mmi(bg[:],
                            wg_sb[p][:, ki * 512 + ot * 128:ki * 512 + (ot + 1) * 128],
                            lnb[:, ki * SC:(ki + 1) * SC], ki == 0,
                            ki == NT - 1)
                    for ki in range(NT):
                        mmi(bu[:],
                            wu_sb[p][:, ki * 512 + ot * 128:ki * 512 + (ot + 1) * 128],
                            lnb[:, ki * SC:(ki + 1) * SC], ki == 0,
                            ki == NT - 1, inc=(ki == NT - 1))
                    rec.wait("scalar", PE, C["pe"])
                    rec.wait("scalar", VE, C["ve"])
                    a6 = aop("activation", sg, bg[:], AF.Silu)
                    rec.wait("vector", AE, a6)
                    pj = (4 * i + ot) % 2
                    rec.wait("vector", QS, fst_free[pj])
                    v_h = vop("tensor_tensor", fstg[pj], sg, bu[:], ALU.mult)
                    rec.wait("sync", VE, v_h)
                    fst_free[pj] = dmas(
                        fh_dr[ob * 512 + ot * 128:ob * 512 + (ot + 1) * 128,
                              i * SC:(i + 1) * SC], fstg[pj])
                wld[ob] = C["pe"]
        barrier()

    if "p3e" in phases:
        cv = Carver()
        NHT = HFF // 128
        fhb = cv.bf16(NHT * SC)
        x1c = cv.f32(NT * SC)
        w2_sb = [cv.bf16(NHT * 128), cv.bf16(NHT * 128)]
        x2s = [cv.f32(SC), cv.f32(SC)]
        wfree = [0, 0]
        st_free = [0, 0]
        for i in range(NOC):
            rec.wait("sync", VE, C["ve"])
            rec.wait("sync", PE, C["pe"])
            fld = dmas(**rl(fhb, fh_dr[:, i * SC:(i + 1) * SC], SC))
            x1ld = dmas(**rl(x1c, x1_dr[:, i * SC:(i + 1) * SC], SC))
            for ot in range(NT):
                p = ot % 2
                rec.wait("gpsimd", PE, wfree[p])
                lw = dma_st("gpsimd", "qw%d" % p,
                            **rl(w2_sb[p], w2p_dr[:, ot * 128:(ot + 1) * 128],
                                 128))
                bi = pb[ot % 4]
                rec.wait("tensor", QS, fld)
                wait_st("tensor", "qw%d" % p, lw)
                rec.wait("tensor", VE, C["ve"])
                for ki in range(NHT):
                    mmi(bi[:], w2_sb[p][:, ki * 128:(ki + 1) * 128],
                        fhb[:, ki * SC:(ki + 1) * SC], ki == 0, ki == NHT - 1,
                        inc=(ki == NHT - 1))
                wfree[p] = C["pe"]
                rec.wait("vector", PE, C["pe"])
                rec.wait("vector", QS, x1ld)
                pj = ot % 2
                rec.wait("vector", QS, st_free[pj])
                v_x2 = vop("scalar_tensor_tensor", x2s[pj], bi[:],
                           float(1.0 / np.sqrt(HFF)),
                           x1c[:, ot * SC:(ot + 1) * SC], ALU.mult, ALU.add)
                rec.wait("sync", VE, v_x2)
                st_free[pj] = dmas(
                    x2o[ot * 128:(ot + 1) * 128, i * SC:(i + 1) * SC],
                    x2s[pj])

    for name, (out_ap, src_ap) in dbg.items():
        dmas(out_ap, src_ap)

    rec.wait("sync", QS, C["qs"])
    rec.wait("gpsimd", QG, C["qg"])

    with nc.Block() as block:
        @block.sync
        def _(e):
            rec.play("sync_pre", e)
            for h in nc.Switch(engines=[e], index=[RT["s_hv"]], n=2):
                RT["h_arm"] = h
                rec.play("sync", e)

        @block.tensor
        def _(e):
            rec.play("tensor", e)

        @block.vector
        def _(e):
            rec.play("vector", e)

        @block.scalar
        def _(e):
            rec.play("scalar", e)

        @block.gpsimd
        def _(e):
            rec.play("gpsimd_pre", e)
            for h in nc.Switch(engines=[e], index=[RT["g_hv"]], n=2):
                RT["h_arm_g"] = h
                rec.play("gpsimd", e)

    es.__exit__(None, None, None)
    return nc



def _hadamard(n):
    h = np.array([[1.0]], dtype=np.float32)
    while h.shape[0] < n:
        h = np.block([[h, h], [h, -h]])
    return np.ascontiguousarray(h.astype(np.float32))


def kernel(L0_attn_hidden_states, L0_attn_in_proj_w, L0_attn_dt_proj_w,
           L0_attn_dt_proj_b, L0_attn_A_log, L0_attn_D_param,
           L0_attn_out_proj_w, L0_attn_gate_w, L0_attn_up_w, L0_attn_down_w,
           L0_attn_norm1_w, L0_attn_norm2_w, L0_ffn_W1, L0_ffn_W2, L0_ffn_W3):
    x0 = np.asarray(L0_attn_hidden_states, dtype=np.float32)
    ipT = np.ascontiguousarray(np.asarray(L0_attn_in_proj_w, np.float32).T)
    dtT = np.ascontiguousarray(np.asarray(L0_attn_dt_proj_w, np.float32).T)
    opT = np.ascontiguousarray(np.asarray(L0_attn_out_proj_w, np.float32).T)
    gT = np.ascontiguousarray(np.asarray(L0_attn_gate_w, np.float32).T)
    uT = np.ascontiguousarray(np.asarray(L0_attn_up_w, np.float32).T)
    dT = np.ascontiguousarray(np.asarray(L0_attn_down_w, np.float32).T)
    w1T = np.ascontiguousarray(np.asarray(L0_ffn_W1, np.float32).T)
    w3T = np.ascontiguousarray(np.asarray(L0_ffn_W3, np.float32).T)
    w2T = np.ascontiguousarray(np.asarray(L0_ffn_W2, np.float32).T)
    dtb = np.asarray(L0_attn_dt_proj_b, np.float32)
    alog = np.asarray(L0_attn_A_log, np.float32)
    dpar = np.asarray(L0_attn_D_param, np.float32)
    n1 = np.asarray(L0_attn_norm1_w, np.float32)
    n2 = np.asarray(L0_attn_norm2_w, np.float32)

    h128 = _hadamard(128)
    h64 = _hadamard(64)
    onesc = np.ones((128, 1), np.float32)
    rng = np.random.default_rng(12345)
    rndf = rng.integers(1 << 20, 1 << 30, size=(1, 16), dtype=np.int32)

    n1T = np.ascontiguousarray(n1.reshape(16, 128).T)
    n2T = np.ascontiguousarray(n2.reshape(16, 128).T)

    x0bs = [np.ascontiguousarray(x0[b].T) for b in range(4)]
    in_maps = []
    for c in range(8):
        b, h = c // 2, c % 2
        wa_ = np.ascontiguousarray(np.concatenate([
            ipT[:, 2048 + 256 * h:2048 + 256 * h + 256],
            ipT[:, 2560 + 256 * h:2560 + 256 * h + 256],
            ipT[:, 3072 + 1024 * h:3072 + 1024 * h + 1024],
            ipT[:, 5120:5248]], axis=1))
        in_maps.append(dict(
            x0b=x0bs[b], wa=wa_,
            wz=np.ascontiguousarray(ipT[:, 0:2048]),
            wdtp=np.ascontiguousarray(dtT[:, 1024 * h:1024 * (h + 1)]),
            wop=opT, wgate=gT, wupp=uT, wdown=dT, w1t=w1T, w3t=w3T, w2t=w2T,
            h128=h128, h64=h64,
            alog=np.ascontiguousarray(alog[1024 * h:1024 * (h + 1)]),
            dtbT=np.ascontiguousarray(
                dtb[1024 * h:1024 * (h + 1)].reshape(8, 128).T),
            dparT=np.ascontiguousarray(
                dpar[1024 * h:1024 * (h + 1)].reshape(8, 128).T),
            n1wT=n1T, n2wT=n2T, onesc=onesc, rndf=rndf))

    nc = build_nc(PHASES, DEBUG)
    res = run_bass_kernel_spmd(nc, in_maps, core_ids=list(range(8)))

    out = np.empty((4, L, D), np.float32)
    for c in range(8):
        b, h = c // 2, c % 2
        x2 = res.results[c]["x2o"]
        for i in range(NOC):
            gi = 2 * i + h
            out[b, gi * SC:(gi + 1) * SC, :] = x2[:, i * SC:(i + 1) * SC].T
    globals()["_last_results"] = res
    return out


# version 04
# speedup vs baseline: 1.4610x; 1.0458x over previous
import contextlib
import numpy as np
import ml_dtypes
import concourse.bass as bass
import concourse.mybir as mybir
from concourse.bass_utils import run_bass_kernel_spmd

F32 = mybir.dt.float32
BF16 = mybir.dt.bfloat16
I32 = mybir.dt.int32
AF = mybir.ActivationFunctionType
ALU = mybir.AluOpType
BF = ml_dtypes.bfloat16

D = 2048
L = 8192
DIN = 2048
NST = 32
DTRK = 128
DFF = 5632
HFF = 8192
EPS = 1e-5

NT = 16
TC = 512
NCK = L // TC
TSC = 1024
NSC = L // TSC
SC = 512
NOC = 4096 // SC
NJT = 8
WA_OF = 1664
WA_OT = 13

DEBUG = ()
PHASES = ("fold", "p01", "p2", "p3a", "p3b", "p3c", "p3d", "p3e")

ENGS = ("sync", "tensor", "vector", "scalar", "gpsimd")


class Rec:
    def __init__(self):
        self.items = {e: [] for e in ENGS}
        self.items["sync_pre"] = []
        self.items["gpsimd_pre"] = []

    def call(self, eng, fn, *a, inc=None, **kw):
        self.items[eng].append((fn, a, kw, inc))

    def raw(self, eng, f):
        self.items[eng].append((None, f, None, None))

    def wait(self, eng, sem, val):
        if val > 0:
            self.call(eng, "wait_ge", sem, val)

    def play(self, name, eng):
        for fn, a, kw, inc in self.items[name]:
            if fn is None:
                a(eng)
            else:
                r = getattr(eng, fn)(*a, **kw)
                if inc is not None:
                    r.then_inc(*inc)


def rl(dst_ap, src_ap, width):
    return dict(out=dst_ap.rearrange("p (i w) -> p i w", w=width),
                in_=src_ap.rearrange("(i p) w -> p i w", p=128))


def build_nc(phases=PHASES, debug=()):
    nc = bass.Bass("TRN2", target_bir_lowering=False, debug=False)

    def din(name, shape, dt=F32):
        return nc.dram_tensor(name, list(shape), dt, kind="ExternalInput").ap()

    x0b = din("x0b", (D, L))
    wa = din("wa", (D, WA_OF))
    wz = din("wz", (D, D))
    wdtp = din("wdtp", (DTRK, 1024))
    wop = din("wop", (DIN, D))
    wgate = din("wgate", (D, DFF))
    wupp = din("wupp", (D, DFF))
    wdown = din("wdown", (DFF, D))
    w1t = din("w1t", (D, HFF))
    w3t = din("w3t", (D, HFF))
    w2t = din("w2t", (HFF, D))
    h128 = din("h128", (128, 128))
    h64 = din("h64", (64, 64))
    alog = din("alog", (1024, NST))
    dtbT = din("dtbT", (128, 8))
    dparT = din("dparT", (128, 8))
    n1wT = din("n1wT", (128, 16))
    n2wT = din("n2wT", (128, 16))
    onesc = din("onesc", (128, 1))
    rndf = din("rndf", (1, 16), I32)

    x2o = nc.dram_tensor("x2o", [D, 4096], F32, kind="ExternalOutput").ap()

    def dt_(name, shape, dt=F32, shared=False):
        return nc.dram_tensor(name, list(shape), dt, kind="Internal",
                              addr_space="Shared" if shared else "Local").ap()

    x_dr = dt_("x_dr", (256, L))
    b_dr = dt_("b_dr", (256, L))
    c_dr = dt_("c_dr", (1024, L))
    dt_dr = dt_("dt_dr", (1024, L))
    sz_dr = dt_("sz_dr", (D, L), BF16)
    mr_dr = dt_("mr_dr", (2, L))
    row_dr = dt_("row_dr", (4, L))
    xn_dr = dt_("xn_dr", (D, 4096), BF16)
    hx_dr = dt_("hx_dr", (D, 4096))
    x1_dr = dt_("x1_dr", (D, 4096))
    ln2_dr = dt_("ln2_dr", (D, 4096), BF16)
    mh_dr = dt_("mh_dr", (DFF, 4096), BF16)
    fh_dr = dt_("fh_dr", (HFF, 4096), BF16)
    m1_dr = dt_("m1_dr", (HFF, D), BF16)
    w2p_dr = dt_("w2p_dr", (HFF, D), BF16)
    yd_sh = dt_("yd_sh", (2, 1024, L), shared=True)
    fl_sh = dt_("fl_sh", (2, 8), I32, shared=True)

    dbg = {}
    for name, src in (("dt", dt_dr), ("bc", b_dr), ("cc", c_dr), ("xs", x_dr),
                      ("sz", sz_dr), ("hx", hx_dr), ("x1", x1_dr),
                      ("ln2", ln2_dr), ("mh", mh_dr), ("fh", fh_dr),
                      ("w2p", w2p_dr), ("xn", xn_dr), ("mr", mr_dr),
                      ("yd", yd_sh)):
        if name in debug:
            dbg[name] = (nc.dram_tensor("dbg_" + name, list(src.shape),
                                        src.dtype, kind="ExternalOutput").ap(),
                         src)

    es = contextlib.ExitStack()
    es.__enter__()

    def sbt(name, p, w, dt=F32):
        return es.enter_context(nc.sbuf_tensor(name, [p, w], dt))

    a_sb = sbt("a_sb", 128, NJT * NST)
    dtb_sb = sbt("dtb_sb", 128, 8)
    dpar_sb = sbt("dpar_sb", 128, 8)
    n1w_sb = sbt("n1w_sb", 128, 16)
    n2w_sb = sbt("n2w_sb", 128, 16)
    ones_sb = sbt("ones_sb", 128, 1, BF16)
    h128_sb = sbt("h128_sb", 128, 128, BF16)
    h64_sb = sbt("h64_sb", 64, 64, BF16)
    carry = sbt("carry", 128, 256)
    rnd_sb = sbt("rnd_sb", 1, 16, I32)
    cb_sb = sbt("cb_sb", 128, 2)
    pf_sb = sbt("pf_sb", 1, 1, I32)
    mrow = sbt("mrow", 1, TC)
    msrow = sbt("msrow", 1, TC)
    vrow = sbt("vrow", 1, TC)
    rrow = sbt("rrow", 1, TC)

    POOLW = 48500
    pool = sbt("pool", 128, POOLW)

    class Carver:
        def __init__(self):
            self.off = 0

        def f32(self, w):
            assert self.off + w <= POOLW, f"pool overflow {self.off}+{w}"
            ap = pool[:, self.off:self.off + w]
            self.off += w
            return ap

        def bf16(self, w):
            assert w % 2 == 0
            return self.f32(w // 2).bitcast(BF16)

    pb = [es.enter_context(nc.psum_tensor(f"pb{i}", [128, 512], F32))
          for i in range(8)]

    sems = {s: es.enter_context(nc.semaphore(s))
            for s in ("qs", "qg", "pe", "ve", "ae", "po", "bar", "ge",
                      "qb0", "qb1", "qc0", "qc1", "qx", "qd", "qy0", "qy1",
                      "qw0", "qw1", "qxn", "qz0", "qz1", "qt0", "qt1")}
    QS, QG, PE, VE, AE, PO, BAR, GE = (sems[s] for s in
                                       ("qs", "qg", "pe", "ve", "ae", "po",
                                        "bar", "ge"))

    rec = Rec()
    C = {"qs": 0, "qg": 0, "pe": 0, "ve": 0, "ae": 0, "bar": 0, "ge": 0}
    RT = {}

    _worig = rec.wait

    def _wait_fix(eng, sem, val):
        if sem is QS:
            val = C["qs"]
        elif sem is QG:
            val = C["qg"]
        _worig(eng, sem, val)
    rec.wait = _wait_fix

    def dmas(out, in_, **kw):
        _worig("sync", QS, C["qs"])
        C["qs"] += 16
        rec.call("sync", "dma_start", out=out, in_=in_, inc=(QS, 16), **kw)
        return C["qs"]

    def dmag(out, in_, **kw):
        _worig("gpsimd", QG, C["qg"])
        C["qg"] += 16
        rec.call("gpsimd", "dma_start", out=out, in_=in_, inc=(QG, 16), **kw)
        return C["qg"]

    def dma_st(queue, skey, out, in_, **kw):
        C[skey] = C.get(skey, 0) + 16
        rec.call(queue, "dma_start", out=out, in_=in_,
                 inc=(sems[skey], 16), **kw)
        return C[skey]

    def wait_st(eng, skey, val):
        _worig(eng, sems[skey], val)

    def vop(fn, *a, **kw):
        _worig("vector", VE, C["ve"])
        C["ve"] += 1
        rec.call("vector", fn, *a, inc=(VE, 1), **kw)
        return C["ve"]

    def aop(fn, *a, **kw):
        _worig("scalar", AE, C["ae"])
        C["ae"] += 1
        rec.call("scalar", fn, *a, inc=(AE, 1), **kw)
        return C["ae"]

    def vopn(fn, *a, wv=0, wa=0, wg=0, ws=(), **kw):
        for k, v in ws:
            _worig("vector", sems[k], v)
        if wv:
            _worig("vector", VE, wv)
        if wa:
            _worig("vector", AE, wa)
        if wg:
            _worig("vector", GE, wg)
        C["ve"] += 1
        rec.call("vector", fn, *a, inc=(VE, 1), **kw)
        return C["ve"]

    def gopn(fn, *a, wv=0, wa=0, wg=0, ws=(), **kw):
        for k, v in ws:
            _worig("gpsimd", sems[k], v)
        if wv:
            _worig("gpsimd", VE, wv)
        if wa:
            _worig("gpsimd", AE, wa)
        if wg:
            _worig("gpsimd", GE, wg)
        C["ge"] += 1
        rec.call("gpsimd", fn, *a, inc=(GE, 1), **kw)
        return C["ge"]

    def aopx(fn, *a, wv=0, wg=0, ws=(), **kw):
        for k, v in ws:
            _worig("scalar", sems[k], v)
        if wv:
            _worig("scalar", VE, wv)
        if wg:
            _worig("scalar", GE, wg)
        return aop(fn, *a, **kw)

    def mmi(out, lhsT, rhs, start, stop, inc=False):
        if inc:
            C["pe"] += 1
            rec.call("tensor", "matmul", out, lhsT, rhs, start=start,
                     stop=stop, inc=(PE, 1))
        else:
            rec.call("tensor", "matmul", out, lhsT, rhs, start=start,
                     stop=stop)
        return C["pe"]

    def barrier():
        rec.wait("sync", QS, C["qs"])
        rec.wait("gpsimd", QG, C["qg"])
        C["bar"] += len(ENGS)
        tgt = C["bar"]
        for e in ENGS:
            rec.call(e, "sem_inc", BAR, 1)
            rec.wait(e, BAR, tgt)

    def pre_sync(e):
        pid = e.partition_id()
        hreg = e.alloc_register("h_sp")
        e.reg_alu(hreg, pid, 2, ALU.mod)
        RT["s_hv"] = bass.RuntimeValue(hreg, min_val=0, max_val=1)

    def pre_gp(e):
        pid = e.partition_id()
        hreg = e.alloc_register("h_gp")
        e.reg_alu(hreg, pid, 2, ALU.mod)
        RT["g_hv"] = bass.RuntimeValue(hreg, min_val=0, max_val=1)
        RT["g_cnt"] = e.alloc_register("pollcnt")
        RT["g_fv"] = e.alloc_register("pollfv")
        RT["g_want"] = e.alloc_register("pollwant")
        e.reg_mov(RT["g_cnt"], 0)

    rec.raw("sync_pre", pre_sync)
    rec.raw("gpsimd_pre", pre_gp)

    def dyn_dma_s(tensor_ap, dims, offs_fn, other, is_store):
        C["qs"] += 16

        wait_prev = C["qs"] - 16

        def run(e, tensor_ap=tensor_ap, dims=dims, offs_fn=offs_fn,
                other=other, is_store=is_store, wait_prev=wait_prev):
            if wait_prev > 0:
                e.wait_ge(QS, wait_prev)
            ap = bass.AP(tensor_ap.tensor, offs_fn(RT["h_arm"]), dims)
            if is_store:
                e.dma_start(out=ap, in_=other).then_inc(QS, 16)
            else:
                e.dma_start(out=other, in_=ap).then_inc(QS, 16)
        rec.raw("sync", run)
        return C["qs"]

    dmag(ones_sb[:], onesc)
    dmag(h128_sb[:], h128)
    dmag(h64_sb[:], h64)
    dmas(dtb_sb[:], dtbT)
    dmas(dpar_sb[:], dparT)
    dmas(n1w_sb[:], n1wT)
    dmas(n2w_sb[:], n2wT)
    dmas(rnd_sb[:], rndf)
    cvc = Carver()
    atmp = cvc.f32(NJT * NST)
    ld = dmas(**rl(atmp, alog, NST))
    rec.wait("scalar", QS, ld)
    aop("activation", a_sb[:], atmp, AF.Exp)
    rec.wait("vector", AE, C["ae"])
    vop("tensor_scalar_mul", a_sb[:], a_sb[:], -1.0)
    vop("memset", carry[:], 0.0)
    vop("memset", cb_sb[:, 0:1], EPS)
    vop("memset", cb_sb[:, 1:2], 1.0)
    nc.const_aps.register(F32, EPS, cb_sb[:, 0:1])
    nc.const_aps.register(F32, 1.0, cb_sb[:, 1:2])
    barrier()

    if "fold" in phases:
        cv = Carver()
        t1b = [cv.bf16(2048), cv.bf16(2048)]
        t2b = [cv.bf16(2048), cv.bf16(2048)]
        stg = [cv.bf16(512) for _ in range(4)]
        stg2 = [cv.bf16(512) for _ in range(4)]
        s1sem = ("qw0", "qw1", "qz0", "qz1")
        s2sem = ("qt0", "qt1", "qy0", "qy1")

        t1_rd = [0, 0]
        stg_st = [C.get(s, 0) for s in s1sem]
        bank_ae = [0, 0, 0, 0]
        ns = 0
        for hl in range(64):
            p = hl % 2
            w2src = bass.AP(w2t.tensor, hl * 2048, [[64 * 2048, 128], [1, 2048]])
            rec.wait("gpsimd", PE, t1_rd[p])
            lda = dma_st("gpsimd", "qd", t1b[p], w2src)
            for dc in range(4):
                b = ns % 4
                bank = pb[b]
                rec.wait("tensor", AE, bank_ae[b])
                wait_st("tensor", "qd", lda)
                mmi(bank[:], h128_sb[:], t1b[p][:, dc * 512:(dc + 1) * 512],
                    True, True, inc=True)
                rec.wait("scalar", PE, C["pe"])
                bank_ae[b] = aopx("copy", stg[b], bank[:],
                                  ws=((s1sem[b], stg_st[b]),))
                m1dst = bass.AP(m1_dr.tensor, hl * 2048 + dc * 512,
                                [[64 * 2048, 128], [1, 512]])
                rec.wait("sync", AE, bank_ae[b])
                stg_st[b] = dma_st("sync", s1sem[b], m1dst, stg[b])
                ns += 1
            t1_rd[p] = C["pe"]
        t2_rd = [0, 0]
        stg2_st = [C.get(s, 0) for s in s2sem]
        bank2_ae = [0, 0, 0, 0]
        ns = 0
        for hh in range(128):
            p = hh % 2
            m1src = bass.AP(m1_dr.tensor, hh * 64 * 2048,
                            [[2048, 64], [1, 2048]])
            rec.wait("gpsimd", PE, t2_rd[p])
            if hh == 0:
                for i in range(4):
                    wait_st("gpsimd", s1sem[i], stg_st[i])
            lda = dma_st("gpsimd", "qx", t2b[p][0:64, :], m1src)
            for dc in range(4):
                b = ns % 4
                bank = pb[4 + b]
                rec.wait("tensor", AE, bank2_ae[b])
                wait_st("tensor", "qx", lda)
                mmi(bank[0:64, :],
                    h64_sb[:], t2b[p][0:64, dc * 512:(dc + 1) * 512],
                    True, True, inc=True)
                rec.wait("scalar", PE, C["pe"])
                bank2_ae[b] = aopx("copy", stg2[b][0:64, :], bank[0:64, :],
                                   ws=((s2sem[b], stg2_st[b]),))
                w2pdst = bass.AP(w2p_dr.tensor, hh * 64 * 2048 + dc * 512,
                                 [[2048, 64], [1, 512]])
                rec.wait("sync", AE, bank2_ae[b])
                stg2_st[b] = dma_st("sync", s2sem[b], w2pdst,
                                    stg2[b][0:64, :])
                ns += 1
            t2_rd[p] = C["pe"]
        for i in range(4):
            wait_st("sync", s2sem[i], stg2_st[i])
        barrier()

    if "p01" in phases:
        cv = Carver()
        wa_sb = cv.bf16(NT * WA_OF)
        wz_sb = cv.bf16(NT * 2048)
        wdt_sb = cv.bf16(1024)
        x0bf = cv.f32(NT * TC)
        ubuf = cv.bf16(NT * TC)
        xb2 = [cv.bf16(TC), cv.bf16(TC)]
        dtrb = cv.bf16(TC)
        stg = [cv.f32(TC), cv.f32(TC), cv.f32(TC)]
        m_b = cv.f32(TC)
        r_b = cv.f32(TC)
        rr_b = cv.f32(TC)
        e1 = cv.f32(TC)
        e2 = cv.f32(TC)

        dmag(**rl(wa_sb, wa, WA_OF))
        dmag(**rl(wz_sb, wz, 2048))
        ldw_done = dmag(wdt_sb, wdtp)

        def wa_st(ki, ot):
            return wa_sb[:, ki * WA_OF + ot * 128:ki * WA_OF + (ot + 1) * 128]

        def wz_st(ki, ot):
            return wz_sb[:, ki * 2048 + ot * 128:ki * 2048 + (ot + 1) * 128]

        slot_dma = [0, 0, 0]
        bank_rel = [0, 0, 0, 0]
        bank_rel_ve = [0, 0, 0, 0]
        for k in range(NCK):
            t0 = k * TC
            rec.wait("sync", VE, C["ve"])
            rec.wait("sync", PE, C["pe"])
            xld = dmas(**rl(x0bf, x0b[:, t0:t0 + TC], TC))
            for ti in range(NT):
                rec.wait("scalar", QS, xld)
                rec.wait("scalar", PE, C["pe"])
                aop("copy", xb2[0], x0bf[:, ti * TC:(ti + 1) * TC])
                a2 = aop("activation", xb2[1], x0bf[:, ti * TC:(ti + 1) * TC],
                         AF.Square)
                rec.wait("tensor", AE, a2)
                if ti == 0:
                    rec.wait("tensor", VE, C["ve"])
                mmi(pb[6][0:1, :], ones_sb[:], xb2[0], ti == 0, ti == NT - 1)
                mmi(pb[7][0:1, :], ones_sb[:], xb2[1], ti == 0, ti == NT - 1,
                    inc=True)
            rec.wait("vector", PE, C["pe"])
            vop("tensor_scalar_mul", mrow[:], pb[6][0:1, :], 1.0 / D)
            vop("tensor_scalar_mul", msrow[:], pb[7][0:1, :], 1.0 / D)
            vop("tensor_tensor", vrow[:], mrow[:], mrow[:], ALU.mult)
            v_var = vop("tensor_tensor", vrow[:], msrow[:], vrow[:],
                        ALU.subtract)
            rec.wait("scalar", VE, v_var)
            aop("activation", rrow[:], vrow[:], AF.Ln, bias=EPS)
            a_r = aop("activation", rrow[:], rrow[:], AF.Exp, scale=-0.5)
            rec.wait("sync", VE, v_var)
            rec.wait("sync", AE, a_r)
            dmas(mr_dr[0:1, t0:t0 + TC], mrow[:])
            dmas(mr_dr[1:2, t0:t0 + TC], rrow[:])
            rec.wait("sync", QS, C["qs"])
            dmas(m_b, mr_dr[0:1, t0:t0 + TC].partition_broadcast(128))
            bc1 = dmas(r_b, mr_dr[1:2, t0:t0 + TC].partition_broadcast(128))
            for ti in range(NT):
                rec.wait("vector", QS, bc1)
                sl = x0bf[:, ti * TC:(ti + 1) * TC]
                vop("tensor_tensor", sl, sl, m_b, ALU.subtract)
                vln = vop("tensor_tensor", sl, sl, r_b, ALU.mult)
                rec.wait("scalar", VE, vln)
                rec.wait("scalar", PE, C["pe"])
                a3 = aop("activation", xb2[ti % 2], sl, AF.Square)
                rec.wait("tensor", AE, a3)
                if ti == 0:
                    rec.wait("tensor", VE, C["ve"])
                mmi(pb[5][0:1, :], ones_sb[:], xb2[ti % 2], ti == 0,
                    ti == NT - 1, inc=True)
            rec.wait("vector", PE, C["pe"])
            v_s2 = vop("tensor_scalar_mul", vrow[:], pb[5][0:1, :], 1.0 / D)
            rec.wait("scalar", VE, v_s2)
            aop("activation", rrow[:], vrow[:], AF.Ln, bias=EPS)
            a_rr = aop("activation", rrow[:], rrow[:], AF.Exp, scale=-0.5)
            rec.wait("sync", AE, a_rr)
            dmas(row_dr[0:1, t0:t0 + TC], rrow[:])
            rec.wait("sync", QS, C["qs"])
            bc2 = dmas(rr_b, row_dr[0:1, t0:t0 + TC].partition_broadcast(128))
            rec.wait("vector", PE, C["pe"])
            for ti in range(NT):
                rec.wait("vector", QS, bc2)
                u_done = vop("scalar_tensor_tensor",
                             ubuf[:, ti * TC:(ti + 1) * TC],
                             x0bf[:, ti * TC:(ti + 1) * TC],
                             n1w_sb[:, ti:ti + 1], rr_b, ALU.mult, ALU.mult)
            rec.wait("tensor", VE, u_done)
            if k == 0:
                rec.wait("tensor", QG, ldw_done)

            def u_mv(ki):
                return ubuf[:, ki * TC:(ki + 1) * TC]

            state = {"g": 0}

            def ogroup(stf, nof, evac):
                for ot in range(nof):
                    bi = state["g"] % 4
                    rec.wait("tensor", AE, bank_rel[bi])
                    rec.wait("tensor", VE, bank_rel_ve[bi])
                    for ki in range(NT):
                        mmi(pb[bi][:], stf(ki, ot), u_mv(ki), ki == 0,
                            ki == NT - 1, inc=(ki == NT - 1))
                    evac(pb[bi], ot, bi)
                    state["g"] += 1

            def evac_wa(bank, ot, bi):
                rec.wait("scalar", PE, C["pe"])
                if ot == 12:
                    bank_rel[bi] = aop("copy", dtrb, bank[:])
                else:
                    si = ot % 3
                    rec.wait("scalar", QS, slot_dma[si])
                    a = aop("copy", stg[si], bank[:])
                    if ot < 2:
                        dst = x_dr[ot * 128:(ot + 1) * 128, t0:t0 + TC]
                    elif ot < 4:
                        dst = b_dr[(ot - 2) * 128:(ot - 1) * 128, t0:t0 + TC]
                    else:
                        dst = c_dr[(ot - 4) * 128:(ot - 3) * 128, t0:t0 + TC]
                    rec.wait("sync", AE, a)
                    slot_dma[si] = dmas(dst, stg[si])
                    bank_rel[bi] = a

            def evac_wz(bank, ot, bi):
                rec.wait("scalar", PE, C["pe"])
                si = ot % 3
                rec.wait("scalar", QS, slot_dma[si])
                s = stg[si].bitcast(BF16)[:, 0:TC]
                a = aop("activation", s, bank[:], AF.Silu)
                rec.wait("sync", AE, a)
                slot_dma[si] = dmas(sz_dr[ot * 128:(ot + 1) * 128, t0:t0 + TC],
                                    s)
                bank_rel[bi] = a

            ogroup(wa_st, WA_OT, evac_wa)
            ogroup(wz_st, NT, evac_wz)
            rec.wait("tensor", AE, C["ae"])
            for ot in range(8):
                bi = state["g"] % 4
                rec.wait("tensor", AE, bank_rel[bi])
                rec.wait("tensor", VE, bank_rel_ve[bi])
                mmi(pb[bi][:], wdt_sb[:, ot * 128:(ot + 1) * 128], dtrb,
                    True, True, inc=True)
                state["g"] += 1
                rec.wait("vector", PE, C["pe"])
                rec.wait("vector", AE, C["ae"])
                v1 = vop("tensor_scalar_add", e1, pb[bi][:],
                         dtb_sb[:, ot:ot + 1])
                bank_rel_ve[bi] = v1
                rec.wait("scalar", VE, v1)
                aop("activation", e2, e1, AF.Abs)
                aop("activation", e2, e2, AF.Exp, scale=-1.0)
                a4 = aop("activation", e2, e2, AF.Ln, bias=1.0)
                rec.wait("vector", AE, a4)
                si = ot % 3
                rec.wait("vector", QS, slot_dma[si])
                v2 = vop("scalar_tensor_tensor", stg[si], e1, 0.0, e2,
                         ALU.max, ALU.add)
                rec.wait("sync", VE, v2)
                slot_dma[si] = dmas(
                    dt_dr[ot * 128:(ot + 1) * 128, t0:t0 + TC], stg[si])
        barrier()

    if "p2" in phases:
        cv = Carver()
        dtb = [cv.f32(TSC), cv.f32(TSC)]
        xb = [cv.f32(TSC), cv.f32(TSC)]
        vslb = [cv.f32(TSC), cv.f32(TSC)]
        dab = [cv.f32(TSC), cv.f32(TSC)]
        ubb = [cv.f32(TSC), cv.f32(TSC)]
        hbb = [cv.f32(TSC), cv.f32(TSC)]
        wbb = [cv.f32(TSC), cv.f32(TSC)]
        ybb = [cv.f32(TSC), cv.f32(TSC)]
        Bb = [cv.f32(TSC), cv.f32(TSC)]
        Cb = [cv.f32(TSC), cv.f32(TSC)]
        ystb = [cv.f32(TSC), cv.f32(TSC)]

        NM = NSC * NJT
        ld_dt = [0, 0]
        ld_x = [0, 0]
        rd_dt_ae = [0, 0]
        rd_dt_ve = [0, 0]
        rd_x_ve = [0, 0]
        rd_vsl_ve = [0, 0]
        exp_ae = [0, 0]
        cc_ae = [0, 0]
        scan_ve = [0, 0]
        ub_ve = [0, 0]
        wb_rel = [("v", 0), ("v", 0)]
        acc_ge = [0, 0]
        bld = [0, 0]
        cld = [0, 0]
        yst_st = [C.get("qy0", 0), C.get("qy1", 0)]
        ysum_ve = 0
        carry_w = [0] * (NJT * NST)

        def ld_B(n, j, ts):
            s = n % 2
            rec.wait("sync", VE, ub_ve[s])
            bsrc = bass.AP(b_dr.tensor, (j * 32 + n) * L + ts,
                           [[0, 128], [1, TSC]])
            bld[s] = dma_st("sync", "qb%d" % s, Bb[s], bsrc)

        def ld_C(n, j, ts):
            s = n % 2
            kind, cnt = wb_rel[s]
            rec.wait("sync", VE if kind == "v" else GE, cnt)
            csrc = bass.AP(c_dr.tensor, (j * 128 + n) * L + ts,
                           [[32 * L, 4], [0, 32], [1, TSC]])
            cld[s] = dma_st("sync", "qc%d" % s, Cb[s], csrc)

        def ld_dtx(m):
            slot = m % 2
            S, j = divmod(m, NJT)
            ts = S * TSC
            rec.wait("sync", AE, rd_dt_ae[slot])
            rec.wait("sync", VE, rd_dt_ve[slot])
            dsrc = bass.AP(dt_dr.tensor, j * 128 * L + ts,
                           [[L, 128], [1, TSC]])
            ld_dt[slot] = dma_st("sync", "qd", dtb[slot], dsrc)
            rec.wait("sync", VE, rd_x_ve[slot])
            xsrc = bass.AP(x_dr.tensor, (j * 32) * L + ts,
                           [[0, 4], [L, 32], [1, TSC]])
            ld_x[slot] = dma_st("sync", "qx", xb[slot], xsrc)

        ld_dtx(0)
        for m in range(NM):
            S, j = divmod(m, NJT)
            ts = S * TSC
            slot = m % 2
            v_vsl = vopn("tensor_tensor", vslb[slot], dtb[slot], xb[slot],
                         ALU.mult,
                         ws=(("qd", ld_dt[slot]), ("qx", ld_x[slot])),
                         wv=rd_vsl_ve[slot])
            ld_B(0, j, ts)
            ld_C(0, j, ts)
            ld_B(1, j, ts)
            ld_C(1, j, ts)
            for t in range(NST + 3):
                par = t % 2
                if t < NST:
                    idx = j * NST + t
                    exp_ae[par] = aopx("activation", dab[par], dtb[slot],
                                       AF.Exp, scale=a_sb[:, idx:idx + 1],
                                       wv=scan_ve[par],
                                       ws=(("qd", ld_dt[slot]),))
                    ub_ve[par] = vopn("tensor_tensor", ubb[par], vslb[slot],
                                      Bb[par], ALU.mult,
                                      ws=(("qb%d" % par, bld[par]),),
                                      wv=max(scan_ve[par], v_vsl))
                n = t - 1
                if 0 <= n < NST:
                    pn = n % 2
                    idx = j * NST + n
                    wk, wc = wb_rel[pn]
                    sv = vopn("tensor_tensor_scan", hbb[pn], dab[pn],
                              ubb[pn], carry[:, idx:idx + 1],
                              ALU.mult, ALU.add,
                              wa=max(exp_ae[pn], cc_ae[pn], carry_w[idx]),
                              wg=wc if wk == "g" else 0,
                              wv=max(ub_ve[pn], wc if wk == "v" else 0))
                    scan_ve[pn] = sv
                    if S < NSC - 1:
                        cc_ae[pn] = aopx("copy", carry[:, idx:idx + 1],
                                         hbb[pn][:, TSC - 1:TSC], wv=sv)
                        carry_w[idx] = cc_ae[pn]
                if t + 2 < NST:
                    ld_B(t + 2, j, ts)
                n = t - 2
                if 0 <= n < NST:
                    pn = n % 2
                    if n % 6 != 5:
                        w = vopn("tensor_tensor", wbb[pn], hbb[pn], Cb[pn],
                                 ALU.mult, ws=(("qc%d" % pn, cld[pn]),),
                                 wv=scan_ve[pn], wg=acc_ge[pn])
                        wb_rel[pn] = ("v", w)
                    else:
                        w = gopn("tensor_tensor", wbb[pn], hbb[pn], Cb[pn],
                                 ALU.mult, ws=(("qc%d" % pn, cld[pn]),),
                                 wv=scan_ve[pn], wg=acc_ge[pn])
                        wb_rel[pn] = ("g", w)
                if 2 <= t < NST:
                    ld_C(t, j, ts)
                n = t - 3
                if 0 <= n < NST:
                    pn = n % 2
                    wk, wc = wb_rel[pn]
                    if n < 2:
                        acc_ge[pn] = gopn(
                            "tensor_copy", ybb[pn], wbb[pn],
                            wv=max(wc if wk == "v" else 0, ysum_ve),
                            wg=wc if wk == "g" else 0)
                    else:
                        acc_ge[pn] = gopn(
                            "tensor_tensor", ybb[pn], ybb[pn], wbb[pn],
                            ALU.add, wv=wc if wk == "v" else 0,
                            wg=max(wc if wk == "g" else 0, acc_ge[pn]))
            if m + 1 < NM:
                ld_dtx(m + 1)
            v_ysum = vopn("tensor_tensor", ystb[slot], ybb[0], ybb[1],
                          ALU.add, wg=max(acc_ge[0], acc_ge[1]),
                          ws=(("qy%d" % slot, yst_st[slot]),))
            ysum_ve = v_ysum
            v_yst = vopn("scalar_tensor_tensor", ystb[slot], xb[slot],
                         dpar_sb[:, j:j + 1], ystb[slot], ALU.mult, ALU.add,
                         wv=v_ysum)
            rd_x_ve[slot] = v_yst
            rd_dt_ae[slot] = exp_ae[1]
            rd_dt_ve[slot] = v_vsl
            rd_vsl_ve[slot] = ub_ve[1]
            rec.wait("sync", VE, v_yst)
            C["qy%d" % slot] += 16
            yst_st[slot] = C["qy%d" % slot]

            def store_yd(e, j=j, ts=ts, slot=slot):
                ap = bass.AP(yd_sh.tensor,
                             RT["h_arm"] * 1024 * L + j * 128 * L + ts,
                             [[L, 128], [1, TSC]])
                e.dma_start(out=ap, in_=ystb[slot]).then_inc(
                    sems["qy%d" % slot], 16)
            rec.raw("sync", store_yd)
            if j == NJT - 1:
                wait_st("sync", "qy0", C.get("qy0", 0))
                wait_st("sync", "qy1", C.get("qy1", 0))
                dyn_dma_s(fl_sh, [[1, 1], [1, 1]],
                          lambda h, S=S: h * 8 + S,
                          rnd_sb[0:1, S:S + 1], True)
        barrier()

    if "p3a" in phases:
        cv = Carver()
        wop_sb = cv.bf16(NT * 2048)
        ydt = [cv.f32(SC), cv.f32(SC)]
        y2b = cv.bf16(NT * SC)
        x0c = cv.f32(NT * SC)
        hbuf = cv.f32(NT * SC)
        ztile = [cv.bf16(SC), cv.bf16(SC)]
        hsq = [cv.bf16(SC), cv.bf16(SC)]
        ttile = cv.f32(SC)
        hxs = [cv.f32(SC), cv.f32(SC)]
        xns = [cv.bf16(SC), cv.bf16(SC)]
        m_b = cv.f32(SC)
        r_b = cv.f32(SC)
        rr_b = cv.f32(SC)

        lwop = dmag(**rl(wop_sb, wop, 2048))
        bank_free = [0, 0, 0, 0]
        hx_dma = [0, 0]
        xn_dma = [0, 0]
        for i in range(NOC):
            def poll(e, i=i):
                cnt = RT["g_cnt"]
                fv = RT["g_fv"]
                want = RT["g_want"]
                oh = 1 - RT["h_arm_g"]
                e.load(want, rnd_sb[0:1, i:i + 1])
                e.reg_add(cnt, cnt, 16)
                e.dma_start(out=pf_sb[:], in_=fl_sh[oh:oh + 1, i:i + 1]
                            ).then_inc(PO, 16)
                e.wait_ge(PO, cnt)
                e.load(fv, pf_sb[0:1, 0:1])
                with e.While(bass.RuntimeValue(fv) != bass.RuntimeValue(want)):
                    e.reg_add(cnt, cnt, 16)
                    e.dma_start(out=pf_sb[:], in_=fl_sh[oh:oh + 1, i:i + 1]
                                ).then_inc(PO, 16)
                    e.wait_ge(PO, cnt)
                    e.load(fv, pf_sb[0:1, 0:1])
                e.sem_inc(QG, 16)
            C["qg"] += 16
            rec.raw("gpsimd", poll)
            pollc = C["qg"]

            rec.wait("sync", QG, pollc)
            rec.wait("sync", VE, C["ve"])
            rec.wait("sync", PE, C["pe"])

            dyn_dma_s(x0b, [[L, 128], [128 * L, 16], [1, SC]],
                      lambda h, i=i: h * 512 + 1024 * i,
                      x0c.rearrange("p (j t) -> p j t", t=SC), False)
            dyn_dma_s(mr_dr, [[0, 128], [1, SC]],
                      lambda h, i=i: h * 512 + 1024 * i, m_b, False)
            ld_main = dyn_dma_s(mr_dr, [[0, 128], [1, SC]],
                                lambda h, i=i: L + h * 512 + 1024 * i,
                                r_b, False)

            zcons = {}
            for ti in range(NT):
                p = ti % 2
                zw = zcons[ti - 2] if ti >= 2 else 0

                def ld_zy(e, i=i, ti=ti, p=p, zw=zw):
                    if zw:
                        e.wait_ge(VE, zw)
                    ap = bass.AP(sz_dr.tensor,
                                 ti * 128 * L + RT["h_arm"] * 512 + 1024 * i,
                                 [[L, 128], [1, SC]])
                    e.dma_start(out=ztile[p], in_=ap).then_inc(
                        sems["qz%d" % p], 16)
                    s_, jj = (0, ti) if ti < 8 else (1, ti - 8)
                    ap2 = bass.AP(yd_sh.tensor,
                                  s_ * 1024 * L + jj * 128 * L
                                  + RT["h_arm"] * 512 + 1024 * i,
                                  [[L, 128], [1, SC]])
                    e.dma_start(out=ydt[p], in_=ap2).then_inc(
                        sems["qt%d" % p], 16)
                C["qz%d" % p] = C.get("qz%d" % p, 0) + 16
                C["qt%d" % p] = C.get("qt%d" % p, 0) + 16
                rec.raw("sync", ld_zy)
                wait_st("vector", "qz%d" % p, C["qz%d" % p])
                wait_st("vector", "qt%d" % p, C["qt%d" % p])
                if ti == 0:
                    rec.wait("vector", QS, ld_main)
                    rec.wait("vector", PE, C["pe"])
                zcons[ti] = vop("tensor_tensor",
                                y2b[:, ti * SC:(ti + 1) * SC],
                                ydt[p], ztile[p],
                                ALU.mult)
            y2done = zcons[NT - 1]

            rec.wait("tensor", VE, y2done)
            if i == 0:
                rec.wait("tensor", QG, lwop)
            for ot in range(NT):
                bi = ot % 4
                rec.wait("tensor", VE, bank_free[bi])
                for ki in range(NT):
                    mmi(pb[bi][:],
                        wop_sb[:, ki * 2048 + ot * 128:ki * 2048 + (ot + 1) * 128],
                        y2b[:, ki * SC:(ki + 1) * SC], ki == 0, ki == NT - 1,
                        inc=(ki == NT - 1))
                x0sl = x0c[:, ot * SC:(ot + 1) * SC]
                hsl = hbuf[:, ot * SC:(ot + 1) * SC]
                vop("tensor_tensor", ttile, x0sl, m_b, ALU.subtract)
                vop("tensor_tensor", ttile, ttile, r_b, ALU.mult)
                rec.wait("vector", PE, C["pe"])
                bank_free[bi] = vop("tensor_tensor", hsl, ttile, pb[bi][:],
                                    ALU.add)
                pj = ot % 2
                rec.wait("vector", QS, hx_dma[pj])
                v_hx = vop("tensor_tensor", hxs[pj], hsl, x0sl, ALU.add)
                rec.wait("sync", VE, v_hx)
                hx_dma[pj] = dmas(
                    hx_dr[ot * 128:(ot + 1) * 128, i * SC:(i + 1) * SC],
                    hxs[pj])
            for ti in range(NT):
                rec.wait("scalar", PE, C["pe"])
                a5 = aop("activation", hsq[ti % 2],
                         hbuf[:, ti * SC:(ti + 1) * SC], AF.Square)
                rec.wait("tensor", AE, a5)
                if ti == 0:
                    rec.wait("tensor", VE, C["ve"])
                mmi(pb[5][0:1, :], ones_sb[:], hsq[ti % 2], ti == 0,
                    ti == NT - 1, inc=True)
            rec.wait("vector", PE, C["pe"])
            v_s2 = vop("tensor_scalar_mul", vrow[:], pb[5][0:1, :], 1.0 / D)
            rec.wait("scalar", VE, v_s2)
            aop("activation", rrow[:], vrow[:], AF.Ln, bias=EPS)
            a_rr = aop("activation", rrow[:], rrow[:], AF.Exp, scale=-0.5)
            rec.wait("sync", AE, a_rr)
            dmas(row_dr[1:2, i * SC:(i + 1) * SC], rrow[:])
            rec.wait("sync", QS, C["qs"])
            bc3 = dmas(rr_b, row_dr[1:2, i * SC:(i + 1) * SC]
                       .partition_broadcast(128))
            for ti in range(NT):
                rec.wait("vector", QS, bc3)
                pj = ti % 2
                rec.wait("vector", QS, xn_dma[pj])
                v_xn = vop("scalar_tensor_tensor", xns[pj],
                           hbuf[:, ti * SC:(ti + 1) * SC],
                           n2w_sb[:, ti:ti + 1], rr_b, ALU.mult, ALU.mult)
                rec.wait("sync", VE, v_xn)
                xn_dma[pj] = dmas(
                    xn_dr[ti * 128:(ti + 1) * 128, i * SC:(i + 1) * SC],
                    xns[pj])
        barrier()

    if "p3b" in phases:
        cv = Carver()
        wg_sb = [cv.bf16(NT * 512), cv.bf16(NT * 512)]
        wu_sb = [cv.bf16(NT * 512), cv.bf16(NT * 512)]
        xnb = cv.bf16(NT * SC)
        sg = cv.f32(SC)
        mstg = [cv.bf16(SC), cv.bf16(SC)]
        NOB = DFF // 512
        wld = {}
        mst_free = [0, 0]
        for ob in range(NOB):
            p = ob % 2
            if ob >= 2:
                rec.wait("gpsimd", PE, wld[ob - 2])
            dma_st("gpsimd", "qw%d" % p,
                   **rl(wg_sb[p], wgate[:, ob * 512:(ob + 1) * 512], 512))
            lw = dma_st("gpsimd", "qw%d" % p,
                        **rl(wu_sb[p], wupp[:, ob * 512:(ob + 1) * 512], 512))
            for i in range(NOC):
                rec.wait("sync", PE, C["pe"])
                xld = dma_st("sync", "qxn",
                             **rl(xnb, xn_dr[:, i * SC:(i + 1) * SC], SC))
                wait_st("tensor", "qxn", xld)
                wait_st("tensor", "qw%d" % p, lw)
                for ot in range(4):
                    bg = pb[ot]
                    bu = pb[4 + ot]
                    rec.wait("tensor", AE, C["ae"])
                    rec.wait("tensor", VE, C["ve"])
                    for ki in range(NT):
                        mmi(bg[:],
                            wg_sb[p][:, ki * 512 + ot * 128:ki * 512 + (ot + 1) * 128],
                            xnb[:, ki * SC:(ki + 1) * SC], ki == 0,
                            ki == NT - 1)
                    for ki in range(NT):
                        mmi(bu[:],
                            wu_sb[p][:, ki * 512 + ot * 128:ki * 512 + (ot + 1) * 128],
                            xnb[:, ki * SC:(ki + 1) * SC], ki == 0,
                            ki == NT - 1, inc=(ki == NT - 1))
                    rec.wait("scalar", PE, C["pe"])
                    rec.wait("scalar", VE, C["ve"])
                    a6 = aop("activation", sg, bg[:], AF.Silu)
                    rec.wait("vector", AE, a6)
                    pj = (4 * i + ot) % 2
                    rec.wait("vector", QS, mst_free[pj])
                    v_h = vop("tensor_tensor", mstg[pj], sg, bu[:], ALU.mult)
                    rec.wait("sync", VE, v_h)
                    mst_free[pj] = dmas(
                        mh_dr[ob * 512 + ot * 128:ob * 512 + (ot + 1) * 128,
                              i * SC:(i + 1) * SC], mstg[pj])
                wld[ob] = C["pe"]
        barrier()

    if "p3c" in phases:
        cv = Carver()
        NHT = DFF // 128
        hidb = cv.bf16(NHT * SC)
        hxb = cv.f32(NT * SC)
        x1b = cv.f32(NT * SC)
        dw_sb = [cv.bf16(NHT * 128), cv.bf16(NHT * 128)]
        xb2c = [cv.bf16(SC), cv.bf16(SC)]
        l2s = [cv.bf16(SC), cv.bf16(SC)]
        x1s = [cv.f32(SC), cv.f32(SC)]
        m_b = cv.f32(SC)
        r_b = cv.f32(SC)
        wfree = [0, 0]
        l2_dma = [0, 0]
        x1_store = 0
        for i in range(NOC):
            rec.wait("sync", VE, C["ve"])
            rec.wait("sync", PE, C["pe"])
            hld = dmas(**rl(hidb, mh_dr[:, i * SC:(i + 1) * SC], SC))
            hxld = dmas(**rl(hxb, hx_dr[:, i * SC:(i + 1) * SC], SC))
            rec.wait("vector", QS, x1_store)
            for ot in range(NT):
                p = ot % 2
                rec.wait("gpsimd", PE, wfree[p])
                lw = dma_st("gpsimd", "qw%d" % p,
                            **rl(dw_sb[p], wdown[:, ot * 128:(ot + 1) * 128],
                                 128))
                bi = pb[ot % 4]
                rec.wait("tensor", QS, hld)
                wait_st("tensor", "qw%d" % p, lw)
                rec.wait("tensor", VE, C["ve"])
                for ki in range(NHT):
                    mmi(bi[:], dw_sb[p][:, ki * 128:(ki + 1) * 128],
                        hidb[:, ki * SC:(ki + 1) * SC], ki == 0, ki == NHT - 1,
                        inc=(ki == NHT - 1))
                wfree[p] = C["pe"]
                rec.wait("vector", PE, C["pe"])
                rec.wait("vector", QS, hxld)
                vop("tensor_tensor", x1b[:, ot * SC:(ot + 1) * SC],
                    hxb[:, ot * SC:(ot + 1) * SC], bi[:], ALU.add)
            for ti in range(NT):
                rec.wait("scalar", PE, C["pe"])
                sl = x1b[:, ti * SC:(ti + 1) * SC]
                aop("copy", xb2c[0], sl)
                a2 = aop("activation", xb2c[1], sl, AF.Square)
                rec.wait("tensor", AE, a2)
                if ti == 0:
                    rec.wait("tensor", VE, C["ve"])
                mmi(pb[6][0:1, :], ones_sb[:], xb2c[0], ti == 0, ti == NT - 1)
                mmi(pb[7][0:1, :], ones_sb[:], xb2c[1], ti == 0, ti == NT - 1,
                    inc=True)
            rec.wait("vector", PE, C["pe"])
            vop("tensor_scalar_mul", mrow[:], pb[6][0:1, :], 1.0 / D)
            vop("tensor_scalar_mul", msrow[:], pb[7][0:1, :], 1.0 / D)
            vop("tensor_tensor", vrow[:], mrow[:], mrow[:], ALU.mult)
            v_var = vop("tensor_tensor", vrow[:], msrow[:], vrow[:],
                        ALU.subtract)
            rec.wait("scalar", VE, v_var)
            aop("activation", rrow[:], vrow[:], AF.Ln, bias=EPS)
            a_r = aop("activation", rrow[:], rrow[:], AF.Exp, scale=-0.5)
            rec.wait("sync", VE, v_var)
            rec.wait("sync", AE, a_r)
            dmas(row_dr[2:3, i * SC:(i + 1) * SC], mrow[:])
            dmas(row_dr[3:4, i * SC:(i + 1) * SC], rrow[:])
            rec.wait("sync", QS, C["qs"])
            dmas(m_b, row_dr[2:3, i * SC:(i + 1) * SC].partition_broadcast(128))
            bc4 = dmas(r_b, row_dr[3:4, i * SC:(i + 1) * SC]
                       .partition_broadcast(128))
            for ti in range(NT):
                pj = ti % 2
                sl = x1b[:, ti * SC:(ti + 1) * SC]
                rec.wait("vector", QS, bc4)
                rec.wait("vector", QS, l2_dma[pj])
                vop("tensor_tensor", x1s[pj], sl, m_b, ALU.subtract)
                v_l2 = vop("tensor_tensor", l2s[pj], x1s[pj], r_b, ALU.mult)
                rec.wait("sync", VE, v_l2)
                l2_dma[pj] = dmas(
                    ln2_dr[ti * 128:(ti + 1) * 128, i * SC:(i + 1) * SC],
                    l2s[pj])
            x1_store = dmas(x1_dr[:, i * SC:(i + 1) * SC]
                            .rearrange("(j p) t -> p j t", p=128),
                            x1b.rearrange("p (j t) -> p j t", t=SC))
        barrier()

    if "p3d" in phases:
        cv = Carver()
        wg_sb = [cv.bf16(NT * 512), cv.bf16(NT * 512)]
        wu_sb = [cv.bf16(NT * 512), cv.bf16(NT * 512)]
        lnb = cv.bf16(NT * SC)
        sg = cv.f32(SC)
        fstg = [cv.bf16(SC), cv.bf16(SC)]
        NOB = HFF // 512
        wld = {}
        fst_free = [0, 0]
        for ob in range(NOB):
            p = ob % 2
            if ob >= 2:
                rec.wait("gpsimd", PE, wld[ob - 2])
            dma_st("gpsimd", "qw%d" % p,
                   **rl(wg_sb[p], w1t[:, ob * 512:(ob + 1) * 512], 512))
            lw = dma_st("gpsimd", "qw%d" % p,
                        **rl(wu_sb[p], w3t[:, ob * 512:(ob + 1) * 512], 512))
            for i in range(NOC):
                rec.wait("sync", PE, C["pe"])
                xld = dma_st("sync", "qxn",
                             **rl(lnb, ln2_dr[:, i * SC:(i + 1) * SC], SC))
                wait_st("tensor", "qxn", xld)
                wait_st("tensor", "qw%d" % p, lw)
                for ot in range(4):
                    bg = pb[ot]
                    bu = pb[4 + ot]
                    rec.wait("tensor", AE, C["ae"])
                    rec.wait("tensor", VE, C["ve"])
                    for ki in range(NT):
                        mmi(bg[:],
                            wg_sb[p][:, ki * 512 + ot * 128:ki * 512 + (ot + 1) * 128],
                            lnb[:, ki * SC:(ki + 1) * SC], ki == 0,
                            ki == NT - 1)
                    for ki in range(NT):
                        mmi(bu[:],
                            wu_sb[p][:, ki * 512 + ot * 128:ki * 512 + (ot + 1) * 128],
                            lnb[:, ki * SC:(ki + 1) * SC], ki == 0,
                            ki == NT - 1, inc=(ki == NT - 1))
                    rec.wait("scalar", PE, C["pe"])
                    rec.wait("scalar", VE, C["ve"])
                    a6 = aop("activation", sg, bg[:], AF.Silu)
                    rec.wait("vector", AE, a6)
                    pj = (4 * i + ot) % 2
                    rec.wait("vector", QS, fst_free[pj])
                    v_h = vop("tensor_tensor", fstg[pj], sg, bu[:], ALU.mult)
                    rec.wait("sync", VE, v_h)
                    fst_free[pj] = dmas(
                        fh_dr[ob * 512 + ot * 128:ob * 512 + (ot + 1) * 128,
                              i * SC:(i + 1) * SC], fstg[pj])
                wld[ob] = C["pe"]
        barrier()

    if "p3e" in phases:
        cv = Carver()
        NHT = HFF // 128
        fhb = cv.bf16(NHT * SC)
        x1c = cv.f32(NT * SC)
        w2_sb = [cv.bf16(NHT * 128), cv.bf16(NHT * 128)]
        x2s = [cv.f32(SC), cv.f32(SC)]
        wfree = [0, 0]
        st_free = [0, 0]
        for i in range(NOC):
            rec.wait("sync", VE, C["ve"])
            rec.wait("sync", PE, C["pe"])
            fld = dmas(**rl(fhb, fh_dr[:, i * SC:(i + 1) * SC], SC))
            x1ld = dmas(**rl(x1c, x1_dr[:, i * SC:(i + 1) * SC], SC))
            for ot in range(NT):
                p = ot % 2
                rec.wait("gpsimd", PE, wfree[p])
                lw = dma_st("gpsimd", "qw%d" % p,
                            **rl(w2_sb[p], w2p_dr[:, ot * 128:(ot + 1) * 128],
                                 128))
                bi = pb[ot % 4]
                rec.wait("tensor", QS, fld)
                wait_st("tensor", "qw%d" % p, lw)
                rec.wait("tensor", VE, C["ve"])
                for ki in range(NHT):
                    mmi(bi[:], w2_sb[p][:, ki * 128:(ki + 1) * 128],
                        fhb[:, ki * SC:(ki + 1) * SC], ki == 0, ki == NHT - 1,
                        inc=(ki == NHT - 1))
                wfree[p] = C["pe"]
                rec.wait("vector", PE, C["pe"])
                rec.wait("vector", QS, x1ld)
                pj = ot % 2
                rec.wait("vector", QS, st_free[pj])
                v_x2 = vop("scalar_tensor_tensor", x2s[pj], bi[:],
                           float(1.0 / np.sqrt(HFF)),
                           x1c[:, ot * SC:(ot + 1) * SC], ALU.mult, ALU.add)
                rec.wait("sync", VE, v_x2)
                st_free[pj] = dmas(
                    x2o[ot * 128:(ot + 1) * 128, i * SC:(i + 1) * SC],
                    x2s[pj])

    for name, (out_ap, src_ap) in dbg.items():
        dmas(out_ap, src_ap)

    rec.wait("sync", QS, C["qs"])
    rec.wait("gpsimd", QG, C["qg"])

    with nc.Block() as block:
        @block.sync
        def _(e):
            rec.play("sync_pre", e)
            for h in nc.Switch(engines=[e], index=[RT["s_hv"]], n=2):
                RT["h_arm"] = h
                rec.play("sync", e)

        @block.tensor
        def _(e):
            rec.play("tensor", e)

        @block.vector
        def _(e):
            rec.play("vector", e)

        @block.scalar
        def _(e):
            rec.play("scalar", e)

        @block.gpsimd
        def _(e):
            rec.play("gpsimd_pre", e)
            for h in nc.Switch(engines=[e], index=[RT["g_hv"]], n=2):
                RT["h_arm_g"] = h
                rec.play("gpsimd", e)

    es.__exit__(None, None, None)
    return nc



def _hadamard(n):
    h = np.array([[1.0]], dtype=np.float32)
    while h.shape[0] < n:
        h = np.block([[h, h], [h, -h]])
    return np.ascontiguousarray(h.astype(np.float32))


def kernel(L0_attn_hidden_states, L0_attn_in_proj_w, L0_attn_dt_proj_w,
           L0_attn_dt_proj_b, L0_attn_A_log, L0_attn_D_param,
           L0_attn_out_proj_w, L0_attn_gate_w, L0_attn_up_w, L0_attn_down_w,
           L0_attn_norm1_w, L0_attn_norm2_w, L0_ffn_W1, L0_ffn_W2, L0_ffn_W3):
    x0 = np.asarray(L0_attn_hidden_states, dtype=np.float32)
    ipT = np.ascontiguousarray(np.asarray(L0_attn_in_proj_w, np.float32).T)
    dtT = np.ascontiguousarray(np.asarray(L0_attn_dt_proj_w, np.float32).T)
    opT = np.ascontiguousarray(np.asarray(L0_attn_out_proj_w, np.float32).T)
    gT = np.ascontiguousarray(np.asarray(L0_attn_gate_w, np.float32).T)
    uT = np.ascontiguousarray(np.asarray(L0_attn_up_w, np.float32).T)
    dT = np.ascontiguousarray(np.asarray(L0_attn_down_w, np.float32).T)
    w1T = np.ascontiguousarray(np.asarray(L0_ffn_W1, np.float32).T)
    w3T = np.ascontiguousarray(np.asarray(L0_ffn_W3, np.float32).T)
    w2T = np.ascontiguousarray(np.asarray(L0_ffn_W2, np.float32).T)
    dtb = np.asarray(L0_attn_dt_proj_b, np.float32)
    alog = np.asarray(L0_attn_A_log, np.float32)
    dpar = np.asarray(L0_attn_D_param, np.float32)
    n1 = np.asarray(L0_attn_norm1_w, np.float32)
    n2 = np.asarray(L0_attn_norm2_w, np.float32)

    h128 = _hadamard(128)
    h64 = _hadamard(64)
    onesc = np.ones((128, 1), np.float32)
    rng = np.random.default_rng(12345)
    rndf = rng.integers(1 << 20, 1 << 30, size=(1, 16), dtype=np.int32)

    n1T = np.ascontiguousarray(n1.reshape(16, 128).T)
    n2T = np.ascontiguousarray(n2.reshape(16, 128).T)

    x0bs = [np.ascontiguousarray(x0[b].T) for b in range(4)]
    in_maps = []
    for c in range(8):
        b, h = c // 2, c % 2
        wa_ = np.ascontiguousarray(np.concatenate([
            ipT[:, 2048 + 256 * h:2048 + 256 * h + 256],
            ipT[:, 2560 + 256 * h:2560 + 256 * h + 256],
            ipT[:, 3072 + 1024 * h:3072 + 1024 * h + 1024],
            ipT[:, 5120:5248]], axis=1))
        in_maps.append(dict(
            x0b=x0bs[b], wa=wa_,
            wz=np.ascontiguousarray(ipT[:, 0:2048]),
            wdtp=np.ascontiguousarray(dtT[:, 1024 * h:1024 * (h + 1)]),
            wop=opT, wgate=gT, wupp=uT, wdown=dT, w1t=w1T, w3t=w3T, w2t=w2T,
            h128=h128, h64=h64,
            alog=np.ascontiguousarray(alog[1024 * h:1024 * (h + 1)]),
            dtbT=np.ascontiguousarray(
                dtb[1024 * h:1024 * (h + 1)].reshape(8, 128).T),
            dparT=np.ascontiguousarray(
                dpar[1024 * h:1024 * (h + 1)].reshape(8, 128).T),
            n1wT=n1T, n2wT=n2T, onesc=onesc, rndf=rndf))

    nc = build_nc(PHASES, DEBUG)
    res = run_bass_kernel_spmd(nc, in_maps, core_ids=list(range(8)))

    out = np.empty((4, L, D), np.float32)
    for c in range(8):
        b, h = c // 2, c % 2
        x2 = res.results[c]["x2o"]
        for i in range(NOC):
            gi = 2 * i + h
            out[b, gi * SC:(gi + 1) * SC, :] = x2[:, i * SC:(i + 1) * SC].T
    globals()["_last_results"] = res
    return out
